# version 12
# baseline (speedup 1.0000x reference)
import numpy as np
from contextlib import ExitStack

import concourse.bass as bass
import concourse.mybir as mybir
from concourse.bass_utils import run_bass_kernel_spmd

F32 = mybir.dt.float32
F16 = mybir.dt.float16
AF = mybir.ActivationFunctionType
ALU = mybir.AluOpType
AX = mybir.AxisListType

D = 2048
H = 16
DH = 128
S = 2048
NB = 4
OWN = 1024
NSLOT = 8
CAP = [2, 4, 6, 8, 10, 12, 14, 16]
TILES = [[1, 2, 5, 6, 9, 10, 13, 14],
         [0, 3, 4, 7, 8, 11, 12, 15]]
KCW = [1024 - 128 * (kc // 2) for kc in range(16)]
POST_OFF = [0]
for _w in KCW:
    POST_OFF.append(POST_OFF[-1] + _w)
POST_W = POST_OFF[-1]
INV_SQRT_DH = 1.0 / float(np.sqrt(np.float64(DH)))
EPS = 1e-5
NEG_BIG = -1e30
FCN = 16
D3 = 3 * D


class Prog:

    ENGS = ("sync", "tensor", "vector", "scalar", "gpsimd")

    def __init__(self):
        self.ops = {e: [] for e in self.ENGS}
        self.count = {}
        self.lastw = {e: {} for e in self.ENGS}
        self.last_self = {}

    def op(self, eng, fn, sem=None, by=1):
        if eng in ("vector", "scalar"):
            self.wait(eng, self.last_self.get(eng))
        mk = None
        if sem is not None:
            self.count[sem] = self.count.get(sem, 0) + by
            mk = (sem, self.count[sem])
            if eng in ("vector", "scalar"):
                self.last_self[eng] = mk
        self.ops[eng].append(("op", fn, sem, by))
        return mk

    def wait(self, eng, mk):
        if mk is None:
            return
        sem, val = mk
        if self.lastw[eng].get(sem, 0) >= val:
            return
        self.lastw[eng][sem] = val
        self.ops[eng].append(("wait", sem, val))

    def waits(self, eng, mks):
        for mk in mks:
            self.wait(eng, mk)


class Buf:

    _n = 0

    def __init__(self, p, weng, newsem):
        self.p = p
        self.weng = weng
        self.ready = None
        self.uses = []
        self.newsem = newsem
        self.sem = None
        self.stsem = None

    def _mksem(self, tag):
        Buf._n += 1
        return self.newsem(f"b{Buf._n}{tag}")

    def begin_write(self):
        self.p.waits(self.weng, self.uses)
        self.uses = []

    def begin_write_on(self, eng):
        self.p.waits(eng, self.uses)
        self.uses = []

    def use(self, mk):
        if mk is not None:
            self.uses.append(mk)

    def load(self, dst, src, split=1):
        if self.sem is None:
            self.sem = self._mksem("l")
        self.begin_write()
        mk = None
        for i in range(split):
            mk = self.p.op("sync", (lambda d_, s_: (lambda e: e.dma_start(
                out=d_, in_=s_)))(dst[i] if split > 1 else dst,
                                  src[i] if split > 1 else src), self.sem, 16)
        self.ready = mk
        return mk

    def store(self, dst, src, deps=()):
        if self.stsem is None:
            self.stsem = self._mksem("s")
        self.p.waits("gpsimd", deps)
        mk = self.p.op("gpsimd", (lambda d_, s_: (lambda e: e.dma_start(
            out=d_, in_=s_)))(dst, src), self.stsem, 16)
        self.uses.append(mk)
        return mk

    def st_total(self):
        return (self.stsem, self.p.count.get(self.stsem, 0))


def build_program():
    nc = bass.Bass("TRN2", target_bir_lowering=False)

    xT = nc.dram_tensor("xT", [D, S], F32, kind="ExternalInput")
    xT_own = nc.dram_tensor("xT_own", [D, OWN], F32, kind="ExternalInput")
    x_own = nc.dram_tensor("x_own", [OWN, D], F32, kind="ExternalInput")
    wqkv = nc.dram_tensor("wqkv", [D, D3], F32, kind="ExternalInput")
    wo = nc.dram_tensor("wo", [D, D], F32, kind="ExternalInput")
    wT = nc.dram_tensor("wT", [D, D], F32, kind="ExternalInput")
    biascd = nc.dram_tensor("biascd", [128, 16], F32, kind="ExternalInput")
    rw1T = nc.dram_tensor("rw1T", [D, 64], F32, kind="ExternalInput")
    rb1d = nc.dram_tensor("rb1d", [64], F32, kind="ExternalInput")
    rw2T16 = nc.dram_tensor("rw2T16", [64, 16], F32, kind="ExternalInput")
    rb2d = nc.dram_tensor("rb2d", [16], F32, kind="ExternalInput")
    gatesd = nc.dram_tensor("gatesd", [16], F32, kind="ExternalInput")
    aT = nc.dram_tensor("aT", [D, 256], F32, kind="ExternalInput")
    bP = nc.dram_tensor("bP", [256, D], F32, kind="ExternalInput")
    posTd = nc.dram_tensor("posTd", [128, POST_W], F16, kind="ExternalInput")
    identd = nc.dram_tensor("identd", [128, 128], F32, kind="ExternalInput")
    outd = nc.dram_tensor("out", [D, OWN], F32, kind="ExternalOutput")

    q_hbm = nc.dram_tensor("q_hbm", [D, OWN], F32)
    y_hbm = nc.dram_tensor("y_hbm", [D, OWN], F32)
    x1pre_hbm = nc.dram_tensor("x1pre_hbm", [OWN, D], F32)
    x1_hbm = nc.dram_tensor("x1_hbm", [OWN, D], F32)
    x2pre_hbm = nc.dram_tensor("x2pre_hbm", [OWN, D], F32)

    p = Prog()
    MM, DV, AC = "s_mm", "s_dv", "s_ac"

    def dap(t, offset, dims):
        return bass.AP(t, offset, dims)

    with ExitStack() as g:
        def sbuf(name, shape, dt=F32):
            return g.enter_context(nc.sbuf_tensor(name, shape, dt))

        sems = {}

        def newsem(name):
            sems[name] = g.enter_context(nc.semaphore(name))
            return name

        for s_ in (MM, DV, AC):
            newsem(s_)

        def B(weng):
            return Buf(p, weng, newsem)

        def phase_barrier(extra=()):
            mks = [(MM, p.count.get(MM, 0)), (DV, p.count.get(DV, 0)),
                   (AC, p.count.get(AC, 0))] + [m for m in extra if m[0]]
            for eng_ in Prog.ENGS:
                p.waits(eng_, mks)

        banks = [g.enter_context(nc.psum_tensor(f"bank{i}", [128, 512], F32))
                 for i in range(8)]
        bankbuf = [B("tensor") for _ in range(8)]

        ident_sb = sbuf("ident_sb", [128, 128])
        rb1_sb = sbuf("rb1_sb", [128, 1])
        rg_sb = sbuf("rg_sb", [128, 16])
        rb2_bc = sbuf("rb2_bc", [128, 16])
        gates_bc = sbuf("gates_bc", [128, 16])
        sums1 = sbuf("sums1", [128, 8])
        sq1 = sbuf("sq1", [128, 8])
        sums2 = sbuf("sums2", [128, 8])
        sq2 = sbuf("sq2", [128, 8])
        mean_sb = sbuf("mean_sb", [128, 8])
        rstd_sb = sbuf("rstd_sb", [128, 8])
        tmp1_sb = sbuf("tmp1_sb", [128, 8])
        tmp2_sb = sbuf("tmp2_sb", [128, 8])
        junk = sbuf("junk", [128, 512])
        acc1_sb = sbuf("acc1_sb", [128, 1])
        ones_sb = sbuf("ones_sb", [128, 128])

        identb, rb1b, rb2b, gatesb = B("sync"), B("sync"), B("sync"), B("sync")
        identmk = identb.load(ident_sb[:, :], identd.ap()[:, :])
        mk_rb1 = rb1b.load(rb1_sb[0:64, 0:1], dap(rb1d, 0, [[1, 64], [1, 1]]))
        mk_rb2 = rb2b.load(rb2_bc[:, :], dap(rb2d, 0, [[0, 128], [1, 16]]))
        mk_g = gatesb.load(gates_bc[:, :], dap(gatesd, 0, [[0, 128], [1, 16]]))
        p.waits("vector", [mk_rb2, mk_g])
        rgmk = p.op("vector", lambda e: e.tensor_add(rg_sb[:, :], rb2_bc[:, :],
                                                     gates_bc[:, :]), DV)
        for buf_ in (sums1, sq1, sums2, sq2):
            p.op("vector", (lambda b_: (lambda e: e.memset(b_[:, :], 0.0)))(buf_), DV)
        onesmk = p.op("vector", lambda e: e.memset(ones_sb[:, :], 1.0), DV)
        zeromk = (DV, p.count[DV])

        with ExitStack() as ph:
            def pb(name, shape, dt=F32):
                return ph.enter_context(nc.sbuf_tensor(name, shape, dt))

            xq = pb("xq", [128, FCN * 1024])
            xqb = B("sync")
            wqb = [pb(f"wqb{i}", [128, FCN * 128]) for i in range(2)]
            wqbuf = [B("sync"), B("sync")]
            qout = [pb(f"qout{i}", [128, 512]) for i in range(2)]
            qoutbuf = [B("vector"), B("vector")]

            xqmk = xqb.load(
                [dap(xq, fc0 * 1024, [[FCN * 1024, 128], [1024, 4], [1, 1024]])
                 for fc0 in range(0, FCN, 4)],
                [dap(xT_own, fc0 * 128 * OWN, [[OWN, 128], [128 * OWN, 4], [1, 1024]])
                 for fc0 in range(0, FCN, 4)], split=4)

            for oc in range(H):
                wb, wbuf = wqb[oc % 2], wqbuf[oc % 2]
                wmk = wbuf.load(
                    [dap(wb, fc0 * 128, [[FCN * 128, 128], [128, 4], [1, 128]])
                     for fc0 in range(0, FCN, 4)],
                    [dap(wqkv, fc0 * 128 * D3 + oc * 128,
                         [[D3, 128], [128 * D3, 4], [1, 128]])
                     for fc0 in range(0, FCN, 4)], split=4)
                for tb in range(2):
                    bi = (oc * 2 + tb) % 8
                    bb = bankbuf[bi]
                    bb.begin_write()
                    p.wait("tensor", wmk)
                    p.wait("tensor", xqmk)
                    for fc in range(FCN):
                        mmk = p.op("tensor", (lambda bi_, fc_, oc_, tb_: (lambda e: e.matmul(
                            banks[bi_][:, 0:512],
                            wqb[oc_ % 2][:, fc_ * 128:(fc_ + 1) * 128],
                            xq[:, fc_ * 1024 + tb_ * 512: fc_ * 1024 + tb_ * 512 + 512],
                            start=(fc_ == 0), stop=(fc_ == FCN - 1))))(bi, fc, oc, tb),
                            MM if fc == FCN - 1 else None)
                    wbuf.use(mmk)
                    bb.ready = mmk
                    qo, qob = qout[tb % 2], qoutbuf[tb % 2]
                    qob.begin_write()
                    p.wait("vector", mmk)
                    emk = p.op("vector", (lambda qo_, bi_: (lambda e: e.tensor_scalar(
                        qo_[:, :], banks[bi_][:, 0:512], INV_SQRT_DH, None,
                        ALU.mult)))(qo, bi), DV)
                    bb.use(emk)
                    qob.store(dap(q_hbm, oc * 128 * OWN + tb * 512,
                                  [[OWN, 128], [1, 512]]),
                              qo[:, :], deps=[emk])
            q_done = [qoutbuf[0].st_total(), qoutbuf[1].st_total()]
        phase_barrier(q_done)

        with ExitStack() as ph:
            def pb(name, shape, dt=F32):
                return ph.enter_context(nc.sbuf_tensor(name, shape, dt))

            xb = [pb(f"xb{i}", [128, FCN * 256]) for i in range(2)]
            xbuf = [B("sync"), B("sync")]
            wk_sb = [pb(f"wk{i}", [128, FCN * 256]) for i in range(2)]
            wv_sb = [pb(f"wv{i}", [128, FCN * 256]) for i in range(2)]
            wkbuf = [B("sync"), B("sync")]
            wvbuf = [B("sync"), B("sync")]
            K2 = [pb(f"K2_{i}", [128, 2 * 2048]) for i in range(2)]
            V2 = [pb(f"V2_{i}", [128, 16 * 256]) for i in range(2)]
            k2buf = [B("vector"), B("vector")]
            v2buf = [B("vector"), B("vector")]
            qh = [pb(f"qh{i}", [128, 1024]) for i in range(2)]
            qhbuf = [B("sync"), B("sync")]
            posb = [pb(f"posb{i}", [128, 1024], F16) for i in range(2)]
            posbuf = [B("sync"), B("sync")]
            st_sb = [pb(f"st_sb{i}", [128, 1024]) for i in range(2)]
            stbuf = [B("vector"), B("vector")]
            pt_sb = [pb(f"pt_sb{i}", [128, 1024]) for i in range(2)]
            ptbuf = [B("scalar"), B("scalar")]
            yfm_sg = pb("yfm_sg", [128, 1024])
            yfmsgbuf = B("vector")

            chunk_ctr = [0]
            kvg_ctr = [0]

            def kv_gen(g_):
                par = g_ % 2
                kbase = D + g_ * 256
                vbase = 2 * D + g_ * 256
                wkmk = wkbuf[par].load(
                    [dap(wk_sb[par], fc0 * 256, [[FCN * 256, 128], [256, 4], [1, 256]])
                     for fc0 in range(0, FCN, 4)],
                    [dap(wqkv, fc0 * 128 * D3 + kbase,
                         [[D3, 128], [128 * D3, 4], [1, 256]])
                     for fc0 in range(0, FCN, 4)], split=4)
                wvmk = wvbuf[par].load(
                    [dap(wv_sb[par], fc0 * 256, [[FCN * 256, 128], [256, 4], [1, 256]])
                     for fc0 in range(0, FCN, 4)],
                    [dap(wqkv, fc0 * 128 * D3 + vbase,
                         [[D3, 128], [128 * D3, 4], [1, 256]])
                     for fc0 in range(0, FCN, 4)], split=4)
                k2buf[par].begin_write()
                v2buf[par].begin_write()
                kv_last = None

                def xload(tb_):
                    return xbuf[tb_ % 2].load(
                        [dap(xb[tb_ % 2], fc0 * 256,
                             [[FCN * 256, 128], [256, 4], [1, 256]])
                         for fc0 in range(0, FCN, 4)],
                        [dap(xT, fc0 * 128 * S + tb_ * 256,
                             [[S, 128], [128 * S, 4], [1, 256]])
                         for fc0 in range(0, FCN, 4)], split=4)

                xmks = [xload(0), xload(1)] + [None] * 6
                yield
                for tb in range(8):
                    xbb = xbuf[tb % 2]
                    xmk = xmks[tb]
                    for oc in range(2):
                        bi = 6 + (kvg_ctr[0] % 2)
                        kvg_ctr[0] += 1
                        bb = bankbuf[bi]
                        bb.begin_write()
                        p.wait("tensor", wkmk)
                        p.wait("tensor", xmk)
                        for fc in range(FCN):
                            mmk = p.op("tensor", (lambda bi_, fc_, oc_, tb_, par_: (lambda e: e.matmul(
                                banks[bi_][:, 0:256],
                                wk_sb[par_][:, fc_ * 256 + oc_ * 128: fc_ * 256 + oc_ * 128 + 128],
                                xb[tb_ % 2][:, fc_ * 256:(fc_ + 1) * 256],
                                start=(fc_ == 0), stop=(fc_ == FCN - 1))))(bi, fc, oc, tb, par),
                                MM if fc == FCN - 1 else None)
                        xbb.use(mmk)
                        wkbuf[par].use(mmk)
                        bb.ready = mmk
                        p.wait("vector", mmk)
                        emk = p.op("vector", (lambda bi_, oc_, tb_, par_: (lambda e: e.tensor_copy(
                            K2[par_][:, oc_ * 2048 + tb_ * 256: oc_ * 2048 + tb_ * 256 + 256],
                            banks[bi_][:, 0:256])))(bi, oc, tb, par), DV)
                        bb.use(emk)
                        kv_last = emk
                        yield
                    for tt in range(2):
                        bi = 6 + (kvg_ctr[0] % 2)
                        kvg_ctr[0] += 1
                        bb = bankbuf[bi]
                        bb.begin_write()
                        p.wait("tensor", wvmk)
                        p.wait("tensor", xmk)
                        for fc in range(FCN):
                            mmk = p.op("tensor", (lambda bi_, fc_, tt_, tb_, par_: (lambda e: e.matmul(
                                banks[bi_][:, 0:256],
                                xb[tb_ % 2][:, fc_ * 256 + tt_ * 128: fc_ * 256 + tt_ * 128 + 128],
                                wv_sb[par_][:, fc_ * 256:(fc_ + 1) * 256],
                                start=(fc_ == 0), stop=(fc_ == FCN - 1))))(bi, fc, tt, tb, par),
                                MM if fc == FCN - 1 else None)
                        xbb.use(mmk)
                        wvbuf[par].use(mmk)
                        bb.ready = mmk
                        p.wait("vector", mmk)
                        emk = p.op("vector", (lambda bi_, tt_, tb_, par_: (lambda e: e.tensor_copy(
                            V2[par_][:, (2 * tb_ + tt_) * 256:(2 * tb_ + tt_) * 256 + 256],
                            banks[bi_][:, 0:256])))(bi, tt, tb, par), DV)
                        bb.use(emk)
                        kv_last = emk
                        if tt == 1 and tb + 2 < 8:
                            xmks[tb + 2] = xload(tb + 2)
                        yield
                k2buf[par].ready = kv_last
                v2buf[par].ready = kv_last

            def emit_pv(par, hd2, kc, exmk):
                s0 = kc // 2
                for half in range(2):
                    lo = max(half * 512, s0 * 128)
                    hi = (half + 1) * 512
                    if lo >= hi:
                        continue
                    wcols = hi - lo
                    ybi = 2 + half
                    dbi = 4 + half
                    last = (kc == 7) if half == 0 else (kc == 15)
                    if kc == 0:
                        bankbuf[ybi].begin_write()
                        bankbuf[dbi].begin_write()
                    p.wait("tensor", exmk)
                    p.wait("tensor", v2buf[par].ready)
                    p.wait("tensor", onesmk)
                    ro = lo - s0 * 128
                    co = lo - half * 512
                    pvmk = p.op("tensor", (lambda ybi_, kc_, hd2_, par_, co_, w_, ro_, st_, sp_: (
                        lambda e: e.matmul(
                            banks[ybi_][:, co_: co_ + w_],
                            V2[par_][:, kc_ * 256 + hd2_ * 128: kc_ * 256 + hd2_ * 128 + 128],
                            pt_sb[kc_ % 2][:, ro_: ro_ + w_],
                            start=st_, stop=sp_)))(ybi, kc, hd2, par, co, wcols,
                                                   ro, kc == 0, last), MM)
                    v2buf[par].use(pvmk)
                    ptbuf[kc % 2].use(pvmk)
                    dmk = p.op("tensor", (lambda dbi_, kc_, co_, w_, ro_, st_, sp_: (
                        lambda e: e.matmul(
                            banks[dbi_][:, co_: co_ + w_],
                            ones_sb[:, :],
                            pt_sb[kc_ % 2][:, ro_: ro_ + w_],
                            start=st_, stop=sp_)))(dbi, kc, co, wcols,
                                                   ro, kc == 0, last), MM)
                    ptbuf[kc % 2].use(dmk)
                    if last:
                        bankbuf[ybi].ready = pvmk
                        bankbuf[dbi].ready = dmk

            def attention(gh, gen):
                par = gh % 2
                yfmsgbuf.begin_write()
                for hd2 in range(2):
                    h = 2 * gh + hd2
                    slope = 2.0 ** (-8.0 / float(h + 1))
                    qbuf = qhbuf[hd2 % 2]
                    p.waits("sync", q_done)
                    qmk = qbuf.load(qh[hd2 % 2][:, :],
                                    dap(q_hbm, h * 128 * OWN, [[OWN, 128], [1, 1024]]))
                    if gen is not None and hd2 == 0:
                        next(gen, None)
                    prev = None
                    for kc in range(16):
                        s0 = kc // 2
                        w = KCW[kc]
                        pbf = posbuf[kc % 2]
                        pmk = pbf.load(posb[kc % 2][:, 0:w],
                                       dap(posTd, POST_OFF[kc], [[POST_W, 128], [1, w]]))
                        stb = stbuf[kc % 2]
                        stb.begin_write_on("scalar")
                        emks = []
                        nch = (w + 511) // 512
                        for ch in range(nch):
                            n = min(512, w - ch * 512)
                            bi = chunk_ctr[0] % 2
                            chunk_ctr[0] += 1
                            bb = bankbuf[bi]
                            bb.begin_write()
                            p.wait("tensor", k2buf[par].ready)
                            p.wait("tensor", qmk)
                            mmk = p.op("tensor", (lambda bi_, kc_, hd2_, par_, s0_, ch_, n_: (
                                lambda e: e.matmul(
                                    banks[bi_][:, 0:n_],
                                    K2[par_][:, hd2_ * 2048 + kc_ * 128: hd2_ * 2048 + kc_ * 128 + 128],
                                    qh[hd2_ % 2][:, s0_ * 128 + ch_ * 512: s0_ * 128 + ch_ * 512 + n_],
                                    start=True, stop=True)))(bi, kc, hd2, par, s0, ch, n),
                                MM)
                            k2buf[par].use(mmk)
                            qbuf.use(mmk)
                            bb.ready = mmk
                            p.wait("scalar", mmk)
                            emk = p.op("scalar", (lambda bi_, kc_, ch_, n_: (lambda e: e.copy(
                                st_sb[kc_ % 2][:, ch_ * 512: ch_ * 512 + n_],
                                banks[bi_][:, 0:n_])))(bi, kc, ch, n), AC)
                            bb.use(emk)
                            emks.append(emk)
                        if gen is not None:
                            next(gen, None)
                        if prev is not None:
                            emit_pv(par, hd2, *prev)
                        p.waits("vector", emks)
                        p.wait("vector", pmk)
                        stmk = p.op("vector", (lambda kc_, w_, sl_: (lambda e: e.scalar_tensor_tensor(
                            st_sb[kc_ % 2][:, 0:w_], posb[kc_ % 2][:, 0:w_], sl_,
                            st_sb[kc_ % 2][:, 0:w_], ALU.mult, ALU.add)))(kc, w, slope), DV)
                        pbf.use(stmk)
                        stb.use(stmk)
                        ptb = ptbuf[kc % 2]
                        ptb.begin_write()
                        p.wait("scalar", stmk)
                        exmk = p.op("scalar", (lambda kc_, w_: (lambda e: e.activation(
                            pt_sb[kc_ % 2][:, 0:w_], st_sb[kc_ % 2][:, 0:w_],
                            AF.Exp)))(kc, w), AC)
                        stb.use(exmk)
                        prev = (kc, exmk)
                    emit_pv(par, hd2, *prev)
                    yfmsgbuf.begin_write()
                    stbuf[0].begin_write_on("vector")
                    for half in range(2):
                        ybi = 2 + half
                        dbi = 4 + half
                        p.wait("vector", bankbuf[dbi].ready)
                        rmk = p.op("vector", (lambda dbi_, half_: (lambda e: e.reciprocal(
                            st_sb[0][:, half_ * 512:(half_ + 1) * 512],
                            banks[dbi_][:, 0:512])))(dbi, half), DV)
                        bankbuf[dbi].use(rmk)
                        stbuf[0].use(rmk)
                        p.wait("vector", bankbuf[ybi].ready)
                        ymk = p.op("vector", (lambda ybi_, half_: (lambda e: e.tensor_mul(
                            yfm_sg[:, half_ * 512: half_ * 512 + 512],
                            banks[ybi_][:, 0:512],
                            st_sb[0][:, half_ * 512:(half_ + 1) * 512])))(ybi, half), DV)
                        bankbuf[ybi].use(ymk)
                        stbuf[0].use(ymk)
                    last_y = (DV, p.count[DV])
                    yfmsgbuf.store(dap(y_hbm, (2 * gh + hd2) * 128 * OWN,
                                       [[OWN, 128], [1, 1024]]),
                                   dap(yfm_sg, 0, [[1024, 128], [1, 1024]]),
                                   deps=[last_y])

            gen0 = kv_gen(0)
            for _ in gen0:
                pass
            for gh in range(8):
                nxt = kv_gen(gh + 1) if gh < 7 else None
                attention(gh, nxt)
                if nxt is not None:
                    for _ in nxt:
                        pass
            y_done = [yfmsgbuf.st_total()]
        phase_barrier(y_done)
        with ExitStack() as cde:
            def cb(name, shape, dt=F32):
                return cde.enter_context(nc.sbuf_tensor(name, shape, dt))

            x1fm = cb("x1fm", [128, FCN * 1024])
            x1fmbuf = B("vector")
            wxaT = cb("wxaT", [128, 2 * 1024])
            wxaTbuf = B("vector")
            coeff = cb("coeff", [128, NSLOT * 16])
            mbc = cb("mbc", [128, 1024])
            rbc = cb("rbc", [128, 1024])
            rstd_tok = cb("rstd_tok", [128, 8])
            vt1 = cb("vt1", [128, 512])
            vt2 = cb("vt2", [128, 512])

            def emit_ln_fm(sumbanks, sqbanks):
                outmks = []
                for tb in range(2):
                    sb_, qb_ = sumbanks[tb], sqbanks[tb]
                    p.wait("vector", bankbuf[sb_].ready)
                    p.op("vector", (lambda sb2, tb_: (lambda e: e.tensor_scalar(
                        mbc[:, tb_ * 512:(tb_ + 1) * 512], banks[sb2][:, 0:512],
                        1.0 / D, None, ALU.mult)))(sb_, tb), DV)
                    mk0 = (DV, p.count[DV])
                    bankbuf[sb_].use(mk0)
                    p.wait("vector", bankbuf[qb_].ready)
                    p.op("vector", (lambda qb2, tb_: (lambda e: e.tensor_scalar(
                        vt1[:, :], banks[qb2][:, 0:512],
                        1.0 / D, None, ALU.mult)))(qb_, tb), DV)
                    mk1 = (DV, p.count[DV])
                    bankbuf[qb_].use(mk1)
                    p.op("vector", (lambda tb_: (lambda e: e.tensor_mul(
                        vt2[:, :], mbc[:, tb_ * 512:(tb_ + 1) * 512],
                        mbc[:, tb_ * 512:(tb_ + 1) * 512])))(tb), DV)
                    p.op("vector", lambda e: e.tensor_sub(vt1[:, :], vt1[:, :],
                                                          vt2[:, :]), DV)
                    vemk = p.op("vector", lambda e: e.tensor_scalar(
                        vt1[:, :], vt1[:, :], EPS, None, ALU.add), DV)
                    p.wait("scalar", vemk)
                    sqmk = p.op("scalar", lambda e: e.activation(
                        vt2[:, :], vt1[:, :], AF.Sqrt), AC)
                    p.wait("vector", sqmk)
                    p.op("vector", (lambda tb_: (lambda e: e.reciprocal(
                        rbc[:, tb_ * 512:(tb_ + 1) * 512], vt2[:, :])))(tb), DV)
                    rsl = (lambda tb_: rbc[:, tb_ * 512:(tb_ + 1) * 512])(tb)
                    mk_ = None
                    for _ in range(2):
                        p.op("vector", (lambda r_: (lambda e: e.tensor_mul(
                            vt2[:, :], r_, r_)))(rsl), DV)
                        p.op("vector", lambda e: e.tensor_mul(vt2[:, :], vt2[:, :],
                                                              vt1[:, :]), DV)
                        p.op("vector", lambda e: e.tensor_scalar(
                            vt2[:, :], vt2[:, :], -0.5, 1.5,
                            ALU.mult, ALU.add), DV)
                        mk_ = p.op("vector", (lambda r_: (lambda e: e.tensor_mul(
                            r_, r_, vt2[:, :])))(rsl), DV)
                    outmks.append(mk_)
                return outmks

            with ExitStack() as ph:
                def pb(name, shape, dt=F32):
                    return ph.enter_context(nc.sbuf_tensor(name, shape, dt))

                yfm_all = pb("yfm_all", [128, FCN * 1024])
                yball = [B("sync"), B("sync")]
                wocol = [pb(f"wocol{i}", [128, FCN * 128]) for i in range(2)]
                wocb = [B("sync"), B("sync")]
                xos = [pb(f"xos{i}", [128, 512]) for i in range(2)]
                xosb = [B("sync"), B("sync")]
                sqs = [pb(f"sqs{i}", [128, 512]) for i in range(2)]
                sqsb = [B("scalar"), B("scalar")]

                p.waits("sync", y_done)
                ymk_h = []
                for th in range(2):
                    mk = yball[th].load(
                        [dap(yfm_all, fc0 * 1024 + th * 512,
                             [[FCN * 1024, 128], [1024, 4], [1, 512]])
                         for fc0 in range(0, FCN, 4)],
                        [dap(y_hbm, fc0 * 128 * OWN + th * 512,
                             [[OWN, 128], [128 * OWN, 4], [1, 512]])
                         for fc0 in range(0, FCN, 4)], split=4)
                    ymk_h.append(mk)
                for b_ in (4, 5, 6, 7):
                    bankbuf[b_].begin_write()
                chmks = []
                for oc in range(FCN):
                    wbuf = wocb[oc % 2]
                    wmk = wbuf.load(
                        [dap(wocol[oc % 2], fc0 * 128,
                             [[FCN * 128, 128], [128, 4], [1, 128]])
                         for fc0 in range(0, FCN, 4)],
                        [dap(wo, fc0 * 128 * D + oc * 128,
                             [[D, 128], [128 * D, 4], [1, 128]])
                         for fc0 in range(0, FCN, 4)], split=4)
                    for tb in range(2):
                        bi = (oc * 2 + tb) % 2
                        bb = bankbuf[bi]
                        bb.begin_write()
                        p.wait("tensor", wmk)
                        p.wait("tensor", ymk_h[tb])
                        for fc in range(FCN):
                            mmk = p.op("tensor", (lambda bi_, fc_, oc_, tb_: (lambda e: e.matmul(
                                banks[bi_][:, 0:512],
                                wocol[oc_ % 2][:, fc_ * 128:(fc_ + 1) * 128],
                                yfm_all[:, fc_ * 1024 + tb_ * 512: fc_ * 1024 + tb_ * 512 + 512],
                                start=(fc_ == 0), stop=(fc_ == FCN - 1))))(bi, fc, oc, tb),
                                MM if fc == FCN - 1 else None)
                        wbuf.use(mmk)
                        yball[tb].use(mmk)
                        bb.ready = mmk
                        xo, xob = xos[tb], xosb[tb]
                        xmk = xob.load(xo[:, :],
                                       dap(xT_own, oc * 128 * OWN + tb * 512,
                                           [[OWN, 128], [1, 512]]))
                        p.wait("vector", mmk)
                        p.wait("vector", xmk)
                        cmk = p.op("vector", (lambda bi_, xo_, oc_, tb_: (lambda e: e.tensor_add(
                            x1fm[:, oc_ * 1024 + tb_ * 512: oc_ * 1024 + tb_ * 512 + 512],
                            banks[bi_][:, 0:512], xo_[:, :])))(bi, xo, oc, tb), DV)
                        bb.use(cmk)
                        xob.use(cmk)
                        chmks.append(cmk)
                        p.wait("tensor", cmk)
                        p.wait("tensor", onesmk)
                        smmk = p.op("tensor", (lambda oc_, tb_: (lambda e: e.matmul(
                            banks[4 + tb_][:, 0:512], ones_sb[:, :],
                            x1fm[:, oc_ * 1024 + tb_ * 512: oc_ * 1024 + tb_ * 512 + 512],
                            start=(oc_ == 0), stop=(oc_ == FCN - 1))))(oc, tb),
                            MM)
                        sq, sqb = sqs[tb], sqsb[tb]
                        sqb.begin_write()
                        p.wait("scalar", cmk)
                        sqmk = p.op("scalar", (lambda sq_, oc_, tb_: (lambda e: e.activation(
                            sq_[:, :],
                            x1fm[:, oc_ * 1024 + tb_ * 512: oc_ * 1024 + tb_ * 512 + 512],
                            AF.Square)))(sq, oc, tb), AC)
                        p.wait("tensor", sqmk)
                        qmmk = p.op("tensor", (lambda sq_, oc_, tb_: (lambda e: e.matmul(
                            banks[6 + tb_][:, 0:512], ones_sb[:, :], sq_[:, :],
                            start=(oc_ == 0), stop=(oc_ == FCN - 1))))(sq, oc, tb),
                            MM)
                        sqb.use(qmmk)
                        if oc == FCN - 1:
                            bankbuf[4 + tb].ready = smmk
                            bankbuf[6 + tb].ready = qmmk
                ln1mks = emit_ln_fm((4, 5), (6, 7))
                for tt in range(NSLOT):
                    bi = tt % 2
                    bb = bankbuf[bi]
                    bb.begin_write()
                    p.wait("tensor", ln1mks[1])
                    p.wait("tensor", identmk)
                    tmk = p.op("tensor", (lambda bi_, tt_: (lambda e: e.transpose(
                        banks[bi_][:, 0:128],
                        rbc[:, tt_ * 128:(tt_ + 1) * 128],
                        ident_sb[:, :])))(bi, tt), MM)
                    bb.ready = tmk
                    p.wait("vector", tmk)
                    rtk = p.op("vector", (lambda bi_, tt_: (lambda e: e.tensor_copy(
                        rstd_tok[:, tt_:tt_ + 1], banks[bi_][:, 0:1])))(bi, tt), DV)
                    bb.use(rtk)
                applymk = None
                for oc in range(FCN):
                    for tb in range(2):
                        p.wait("vector", ln1mks[tb])
                        applymk = p.op("vector", (lambda oc_, tb_: (lambda e: e.tensor_sub(
                            x1fm[:, oc_ * 1024 + tb_ * 512: oc_ * 1024 + tb_ * 512 + 512],
                            x1fm[:, oc_ * 1024 + tb_ * 512: oc_ * 1024 + tb_ * 512 + 512],
                            mbc[:, tb_ * 512:(tb_ + 1) * 512])))(oc, tb), DV)
                x1fmbuf.ready = applymk
            phase_barrier()

            with ExitStack() as ph:
                def pb(name, shape, dt=F32):
                    return ph.enter_context(nc.sbuf_tensor(name, shape, dt))

                rw1_sb = pb("rw1_sb", [128, FCN * 64])
                rw2_sb = pb("rw2_sb", [64, 16])
                h_fm = pb("h_fm", [64, 1024])
                logits = pb("logits", [128, 16])
                maskedL = pb("maskedL", [128, 16])
                eq1 = pb("eq1", [128, 16])
                eq2 = pb("eq2", [128, 16])
                m1s = pb("m1s", [128, 1])
                nm1 = pb("nm1", [128, 1])
                m2s = pb("m2s", [128, 1])
                e2s = pb("e2s", [128, 1])
                p1s = pb("p1s", [128, 1])
                p2s = pb("p2s", [128, 1])
                aT_sb = pb("aT_sb", [128, FCN * 256])
                xa_sb = pb("xa_sb", [128, 256])
                wxa_sb = pb("wxa_sb", [128, 256])
                hsc = pb("hsc", [128, 512])
                rw1b, rw2b, aTb = B("sync"), B("sync"), B("sync")

                r1mk = rw1b.load(
                    [dap(rw1_sb, fc0 * 64, [[FCN * 64, 128], [64, 4], [1, 64]])
                     for fc0 in range(0, FCN, 4)],
                    [dap(rw1T, fc0 * 128 * 64, [[64, 128], [128 * 64, 4], [1, 64]])
                     for fc0 in range(0, FCN, 4)], split=4)
                r2mk = rw2b.load(rw2_sb[:, :], dap(rw2T16, 0, [[16, 64], [1, 16]]))
                atmk = aTb.load(
                    [dap(aT_sb, fc0 * 256, [[FCN * 256, 128], [256, 4], [1, 256]])
                     for fc0 in range(0, FCN, 4)],
                    [dap(aT, fc0 * 128 * 256, [[256, 128], [128 * 256, 4], [1, 256]])
                     for fc0 in range(0, FCN, 4)], split=4)

                h_evacs = []
                for tb in range(2):
                    bi = 6 + tb % 2
                    bb = bankbuf[bi]
                    bb.begin_write()
                    p.wait("tensor", r1mk)
                    p.wait("tensor", x1fmbuf.ready)
                    for fc in range(FCN):
                        mmk = p.op("tensor", (lambda bi_, fc_, tb_: (lambda e: e.matmul(
                            banks[bi_][0:64, 0:512],
                            rw1_sb[:, fc_ * 64:(fc_ + 1) * 64],
                            x1fm[:, fc_ * 1024 + tb_ * 512: fc_ * 1024 + tb_ * 512 + 512],
                            start=(fc_ == 0), stop=(fc_ == FCN - 1))))(bi, fc, tb),
                            MM if fc == FCN - 1 else None)
                    x1fmbuf.use(mmk)
                    bb.ready = mmk
                    p.wait("vector", mmk)
                    if h_evacs:
                        p.wait("vector", h_evacs[-1])
                    hmmk = p.op("vector", (lambda bi_, tb_: (lambda e: e.tensor_mul(
                        hsc[0:64, :], banks[bi_][0:64, 0:512],
                        rbc[0:64, tb_ * 512:(tb_ + 1) * 512])))(bi, tb), DV)
                    bb.use(hmmk)
                    p.wait("scalar", hmmk)
                    p.wait("scalar", mk_rb1)
                    emk = p.op("scalar", (lambda tb_: (lambda e: e.activation(
                        h_fm[:, tb_ * 512:(tb_ + 1) * 512],
                        hsc[0:64, :], AF.Silu,
                        bias=rb1_sb[0:64, :], scale=1.0)))(tb), AC)
                    h_evacs.append(emk)

                cmk = None
                for tt in range(NSLOT):
                    bi = tt % 2
                    bb = bankbuf[bi]
                    bb.begin_write()
                    p.waits("tensor", h_evacs)
                    p.wait("tensor", r2mk)
                    lmk = p.op("tensor", (lambda bi_, tt_: (lambda e: e.matmul(
                        banks[bi_][:, 0:16],
                        h_fm[:, tt_ * 128:(tt_ + 1) * 128],
                        rw2_sb[:, :], start=True, stop=True)))(bi, tt), MM)
                    bb.ready = lmk
                    p.wait("vector", lmk)
                    p.wait("vector", rgmk)
                    lgmk = p.op("vector", (lambda bi_: (lambda e: e.tensor_add(
                        logits[:, :], banks[bi_][:, 0:16], rg_sb[:, :])))(bi), DV)
                    bb.use(lgmk)
                    p.op("vector", lambda e: e.reduce_max(nm1[:, :], logits[:, :],
                                                          axis=AX.X, negate=True), DV)
                    p.op("vector", lambda e: e.tensor_scalar(
                        m1s[:, :], nm1[:, :], -1.0, None, ALU.mult), DV)
                    p.op("vector", lambda e: e.tensor_scalar(
                        eq1[:, :], logits[:, :], m1s[:, :], None, ALU.is_equal), DV)
                    p.op("vector", lambda e: e.scalar_tensor_tensor(
                        maskedL[:, :], eq1[:, :], NEG_BIG, logits[:, :],
                        ALU.mult, ALU.add), DV)
                    m2mk = p.op("vector", lambda e: e.reduce_max(
                        m2s[:, :], maskedL[:, :], axis=AX.X), DV)
                    p.wait("scalar", m2mk)
                    e2mk = p.op("scalar", lambda e: e.activation(
                        e2s[:, :], m2s[:, :], AF.Exp, bias=nm1[:, :], scale=1.0), AC)
                    p.wait("vector", e2mk)
                    p.op("vector", lambda e: e.tensor_scalar(
                        p1s[:, :], e2s[:, :], 1.0, None, ALU.add), DV)
                    p.op("vector", lambda e: e.reciprocal(p1s[:, :], p1s[:, :]), DV)
                    p.op("vector", lambda e: e.tensor_mul(p2s[:, :], e2s[:, :],
                                                          p1s[:, :]), DV)
                    p.op("vector", lambda e: e.tensor_scalar(
                        eq2[:, :], maskedL[:, :], m2s[:, :], None, ALU.is_equal), DV)
                    p.op("vector", lambda e: e.tensor_scalar(
                        eq1[:, :], eq1[:, :], p1s[:, :], None, ALU.mult), DV)
                    cmk = p.op("vector", (lambda tt_: (lambda e: e.scalar_tensor_tensor(
                        coeff[:, tt_ * 16:(tt_ + 1) * 16], eq2[:, :], p2s[:, :],
                        eq1[:, :], ALU.mult, ALU.add)))(tt), DV)

                wxa_last = None
                for tt in range(NSLOT):
                    bi = 2 + tt % 2
                    bb = bankbuf[bi]
                    bb.begin_write()
                    p.wait("tensor", atmk)
                    p.wait("tensor", x1fmbuf.ready)
                    for fc in range(FCN):
                        mmk = p.op("tensor", (lambda bi_, fc_, tt_: (lambda e: e.matmul(
                            banks[bi_][:, 0:256],
                            x1fm[:, fc_ * 1024 + tt_ * 128: fc_ * 1024 + tt_ * 128 + 128],
                            aT_sb[:, fc_ * 256:(fc_ + 1) * 256],
                            start=(fc_ == 0), stop=(fc_ == FCN - 1))))(bi, fc, tt),
                            MM if fc == FCN - 1 else None)
                    x1fmbuf.use(mmk)
                    bb.ready = mmk
                    p.wait("vector", mmk)
                    xamk = p.op("vector", (lambda bi_, tt_: (lambda e: e.tensor_scalar(
                        xa_sb[:, :], banks[bi_][:, 0:256],
                        rstd_tok[:, tt_:tt_ + 1], None, ALU.mult)))(bi, tt), DV)
                    bb.use(xamk)
                    p.wait("vector", cmk)
                    wmk = None
                    for ex in range(16):
                        wmk = p.op("vector", (lambda ex_, tt_: (lambda e: e.tensor_scalar(
                            wxa_sb[:, ex_ * 16:(ex_ + 1) * 16],
                            xa_sb[:, ex_ * 16:(ex_ + 1) * 16],
                            coeff[:, tt_ * 16 + ex_: tt_ * 16 + ex_ + 1],
                            None, ALU.mult)))(ex, tt), DV)
                    for c in range(2):
                        bi2 = 4 + c
                        bb2 = bankbuf[bi2]
                        bb2.begin_write()
                        p.wait("tensor", wmk)
                        p.wait("tensor", identmk)
                        tmk = p.op("tensor", (lambda bi2_, c_: (lambda e: e.transpose(
                            banks[bi2_][:, 0:128],
                            wxa_sb[:, c_ * 128:(c_ + 1) * 128],
                            ident_sb[:, :])))(bi2, c), MM)
                        bb2.ready = tmk
                        p.wait("vector", tmk)
                        emk = p.op("vector", (lambda bi2_, c_, tt_: (lambda e: e.tensor_copy(
                            wxaT[:, c_ * 1024 + tt_ * 128: c_ * 1024 + tt_ * 128 + 128],
                            banks[bi2_][:, 0:128])))(bi2, c, tt), DV)
                        bb2.use(emk)
                        wxa_last = emk
                wxaTbuf.ready = wxa_last
            phase_barrier()

            with ExitStack() as ph:
                def pb(name, shape, dt=F32):
                    return ph.enter_context(nc.sbuf_tensor(name, shape, dt))

                x2fm = pb("x2fm", [128, FCN * 1024])
                wtcol = [pb(f"wtcol{i}", [128, FCN * 128]) for i in range(2)]
                wtcb = [B("sync"), B("sync")]
                bP_sb = pb("bP_sb", [128, 2 * 2048])
                bias_v = pb("bias_v", [128, 16])
                sqs = [pb(f"sqsE{i}", [128, 512]) for i in range(2)]
                sqsb = [B("scalar"), B("scalar")]
                x2buf = B("vector")
                bPb, biasb = B("sync"), B("sync")

                bpmk = bPb.load([dap(bP_sb, c * 2048, [[2 * 2048, 128], [1, 2048]])
                                 for c in range(2)],
                                [dap(bP, c * 128 * D, [[D, 128], [1, 2048]])
                                 for c in range(2)], split=2)
                bvmk = biasb.load(bias_v[:, :], dap(biascd, 0, [[16, 128], [1, 16]]))

                for b_ in (4, 5, 6, 7):
                    bankbuf[b_].begin_write()
                for oc in range(FCN):
                    wbuf = wtcb[oc % 2]
                    wmk = wbuf.load(
                        [dap(wtcol[oc % 2], fc0 * 128,
                             [[FCN * 128, 128], [128, 4], [1, 128]])
                         for fc0 in range(0, FCN, 4)],
                        [dap(wT, fc0 * 128 * D + oc * 128,
                             [[D, 128], [128 * D, 4], [1, 128]])
                         for fc0 in range(0, FCN, 4)], split=4)
                    for tb in range(2):
                        bi = (oc * 2 + tb) % 2
                        bb = bankbuf[bi]
                        bb.begin_write()
                        p.wait("tensor", wmk)
                        p.wait("tensor", x1fmbuf.ready)
                        p.wait("tensor", wxaTbuf.ready)
                        p.wait("tensor", bpmk)
                        for fc in range(FCN):
                            mmk = p.op("tensor", (lambda bi_, fc_, oc_, tb_: (lambda e: e.matmul(
                                banks[bi_][:, 0:512],
                                wtcol[oc_ % 2][:, fc_ * 128:(fc_ + 1) * 128],
                                x1fm[:, fc_ * 1024 + tb_ * 512: fc_ * 1024 + tb_ * 512 + 512],
                                start=(fc_ == 0), stop=(fc_ == FCN - 1))))(bi, fc, oc, tb),
                                MM if fc == FCN - 1 else None)
                        bb.ready = mmk
                        wbuf.use(mmk)
                        x1fmbuf.use(mmk)
                        di = 2 + bi
                        db = bankbuf[di]
                        db.begin_write()
                        p.wait("tensor", wxaTbuf.ready)
                        p.wait("tensor", bpmk)
                        for c in range(2):
                            dmk = p.op("tensor", (lambda di_, c_, oc_, tb_: (lambda e: e.matmul(
                                banks[di_][:, 0:512],
                                bP_sb[:, c_ * 2048 + oc_ * 128: c_ * 2048 + oc_ * 128 + 128],
                                wxaT[:, c_ * 1024 + tb_ * 512: c_ * 1024 + tb_ * 512 + 512],
                                start=(c_ == 0), stop=(c_ == 1))))(di, c, oc, tb),
                                MM if c == 1 else None)
                        db.ready = dmk
                        wxaTbuf.use(dmk)
                        p.wait("vector", mmk)
                        p.wait("vector", dmk)
                        p.wait("vector", bvmk)
                        p.op("vector", (lambda bi_, oc_, tb_: (lambda e: e.tensor_add(
                            x2fm[:, oc_ * 1024 + tb_ * 512: oc_ * 1024 + tb_ * 512 + 512],
                            banks[bi_][:, 0:512],
                            x1fm[:, oc_ * 1024 + tb_ * 512: oc_ * 1024 + tb_ * 512 + 512])))(bi, oc, tb), DV)
                        bb.use((DV, p.count[DV]))
                        p.op("vector", (lambda oc_, tb_: (lambda e: e.tensor_mul(
                            x2fm[:, oc_ * 1024 + tb_ * 512: oc_ * 1024 + tb_ * 512 + 512],
                            x2fm[:, oc_ * 1024 + tb_ * 512: oc_ * 1024 + tb_ * 512 + 512],
                            rbc[:, tb_ * 512:(tb_ + 1) * 512])))(oc, tb), DV)
                        cmk = p.op("vector", (lambda di_, oc_, tb_: (lambda e: e.scalar_tensor_tensor(
                            x2fm[:, oc_ * 1024 + tb_ * 512: oc_ * 1024 + tb_ * 512 + 512],
                            banks[di_][:, 0:512], bias_v[:, oc_:oc_ + 1],
                            x2fm[:, oc_ * 1024 + tb_ * 512: oc_ * 1024 + tb_ * 512 + 512],
                            ALU.add, ALU.add)))(di, oc, tb), DV)
                        db.use(cmk)
                        x2buf.use(cmk)
                        p.wait("tensor", cmk)
                        p.wait("tensor", onesmk)
                        smmk = p.op("tensor", (lambda oc_, tb_: (lambda e: e.matmul(
                            banks[4 + tb_][:, 0:512], ones_sb[:, :],
                            x2fm[:, oc_ * 1024 + tb_ * 512: oc_ * 1024 + tb_ * 512 + 512],
                            start=(oc_ == 0), stop=(oc_ == FCN - 1))))(oc, tb),
                            MM)
                        sq, sqb = sqs[tb], sqsb[tb]
                        sqb.begin_write()
                        p.wait("scalar", cmk)
                        sqmk = p.op("scalar", (lambda sq_, oc_, tb_: (lambda e: e.activation(
                            sq_[:, :],
                            x2fm[:, oc_ * 1024 + tb_ * 512: oc_ * 1024 + tb_ * 512 + 512],
                            AF.Square)))(sq, oc, tb), AC)
                        p.wait("tensor", sqmk)
                        qmmk = p.op("tensor", (lambda sq_, oc_, tb_: (lambda e: e.matmul(
                            banks[6 + tb_][:, 0:512], ones_sb[:, :], sq_[:, :],
                            start=(oc_ == 0), stop=(oc_ == FCN - 1))))(sq, oc, tb),
                            MM)
                        sqb.use(qmmk)
                        if oc == FCN - 1:
                            bankbuf[4 + tb].ready = smmk
                            bankbuf[6 + tb].ready = qmmk
                ln2mks = emit_ln_fm((4, 5), (6, 7))
                for oc in range(FCN):
                    for tb in range(2):
                        p.wait("vector", ln2mks[tb])
                        p.op("vector", (lambda oc_, tb_: (lambda e: e.tensor_sub(
                            x2fm[:, oc_ * 1024 + tb_ * 512: oc_ * 1024 + tb_ * 512 + 512],
                            x2fm[:, oc_ * 1024 + tb_ * 512: oc_ * 1024 + tb_ * 512 + 512],
                            mbc[:, tb_ * 512:(tb_ + 1) * 512])))(oc, tb), DV)
                        applymk = p.op("vector", (lambda oc_, tb_: (lambda e: e.tensor_mul(
                            x2fm[:, oc_ * 1024 + tb_ * 512: oc_ * 1024 + tb_ * 512 + 512],
                            x2fm[:, oc_ * 1024 + tb_ * 512: oc_ * 1024 + tb_ * 512 + 512],
                            rbc[:, tb_ * 512:(tb_ + 1) * 512])))(oc, tb), DV)
                out_mks = []
                for q4 in range(4):
                    mk = x2buf.store(
                        dap(outd, q4 * 4 * 128 * OWN, [[OWN, 128], [128 * OWN, 4], [1, 1024]]),
                        dap(x2fm, q4 * 4 * 1024, [[FCN * 1024, 128], [1024, 4], [1, 1024]]),
                        deps=[applymk])
                out_done = [x2buf.st_total()]

        for sem_, val_ in out_done:
            p.ops["gpsimd"].append(("wait", sem_, val_))

        block = g.enter_context(nc.Block())

        def make_runner(eng_name):
            lst = p.ops[eng_name]

            def run(e):
                for item in lst:
                    if item[0] == "wait":
                        e.wait_ge(sems[item[1]], item[2])
                    else:
                        _, fn, sem, by = item
                        ins = fn(e)
                        if sem is not None:
                            ins.then_inc(sems[sem], by)
            return run

        block.sync(make_runner("sync"))
        block.tensor(make_runner("tensor"))
        block.vector(make_runner("vector"))
        block.scalar(make_runner("scalar"))
        block.gpsimd(make_runner("gpsimd"))

    return nc


_NC_CACHE = []


def _build_posT(r):
    pos = np.full((128, POST_W), -60000.0, dtype=np.float32)
    tiles = np.asarray(TILES[r])
    for kc in range(16):
        s0 = kc // 2
        w = KCW[kc]
        k = (kc * 128 + np.arange(128))[:, None]
        j = np.arange(w)[None, :]
        scol = s0 + j // 128
        q = tiles[scol] * 128 + (j % 128)
        blockv = np.where(k <= q, (k - q).astype(np.float32), -60000.0)
        pos[:, POST_OFF[kc]:POST_OFF[kc] + w] = blockv
    return pos.astype(np.float16)


def kernel(L0_attn_x, L0_attn_W_qkv, L0_attn_W_o, L0_ffn_W, L0_ffn_bias,
           L0_ffn_rw1, L0_ffn_rb1, L0_ffn_rw2, L0_ffn_rb2, L0_ffn_A,
           L0_ffn_B, L0_ffn_gates):
    x = np.asarray(L0_attn_x, dtype=np.float32)
    wqkv = np.ascontiguousarray(np.asarray(L0_attn_W_qkv, dtype=np.float32))
    wo = np.ascontiguousarray(np.asarray(L0_attn_W_o, dtype=np.float32))
    wT = np.ascontiguousarray(np.asarray(L0_ffn_W, dtype=np.float32).T)
    bias_cols = np.ascontiguousarray(
        np.asarray(L0_ffn_bias, dtype=np.float32).reshape(16, 128).T)
    rw1T = np.ascontiguousarray(np.asarray(L0_ffn_rw1, dtype=np.float32).T)
    rb1 = np.ascontiguousarray(np.asarray(L0_ffn_rb1, dtype=np.float32))
    rw2T16 = np.ascontiguousarray(np.asarray(L0_ffn_rw2, dtype=np.float32).T[:, :16])
    rb2_16 = np.ascontiguousarray(np.asarray(L0_ffn_rb2, dtype=np.float32)[:16])
    aT = np.ascontiguousarray(np.asarray(L0_ffn_A, dtype=np.float32).reshape(256, D).T)
    bP = np.ascontiguousarray(
        np.asarray(L0_ffn_B, dtype=np.float32).transpose(0, 2, 1).reshape(256, D))
    gates = np.ascontiguousarray(np.asarray(L0_ffn_gates, dtype=np.float32))
    ident = np.eye(128, dtype=np.float32)

    posT_r = [_build_posT(0), _build_posT(1)]
    own_rows_r = []
    for r in range(2):
        rows = np.concatenate([np.arange(128 * t, 128 * t + 128) for t in TILES[r]])
        own_rows_r.append(rows)

    if not _NC_CACHE:
        _NC_CACHE.append(build_program())
    nc = _NC_CACHE[0]

    in_maps = []
    for c in range(8):
        b, r = c // 2, c % 2
        xb = np.ascontiguousarray(x[b])
        xTb = np.ascontiguousarray(xb.T)
        rows = own_rows_r[r]
        in_maps.append({
            "xT": xTb,
            "xT_own": np.ascontiguousarray(xTb[:, rows]),
            "x_own": np.ascontiguousarray(xb[rows]),
            "wqkv": wqkv, "wo": wo, "wT": wT, "biascd": bias_cols,
            "rw1T": rw1T, "rb1d": rb1, "rw2T16": rw2T16, "rb2d": rb2_16,
            "gatesd": gates, "aT": aT, "bP": bP,
            "posTd": posT_r[r], "identd": ident,
        })

    res = run_bass_kernel_spmd(nc, in_maps, core_ids=list(range(8)))

    out = np.empty((NB, S, D), dtype=np.float32)
    for c in range(8):
        b, r = c // 2, c % 2
        co = np.asarray(res.results[c]["out"]).T
        for s_, tile in enumerate(TILES[r]):
            out[b, 128 * tile:128 * tile + 128, :] = co[128 * s_:128 * (s_ + 1), :]
    return out


# version 13
# speedup vs baseline: 1.0007x; 1.0007x over previous
import numpy as np
from contextlib import ExitStack

import concourse.bass as bass
import concourse.mybir as mybir
from concourse.bass_utils import run_bass_kernel_spmd

F32 = mybir.dt.float32
F16 = mybir.dt.float16
AF = mybir.ActivationFunctionType
ALU = mybir.AluOpType
AX = mybir.AxisListType

D = 2048
H = 16
DH = 128
S = 2048
NB = 4
OWN = 1024
NSLOT = 8
CAP = [2, 4, 6, 8, 10, 12, 14, 16]
TILES = [[1, 2, 5, 6, 9, 10, 13, 14],
         [0, 3, 4, 7, 8, 11, 12, 15]]
KCW = [1024 - 128 * (kc // 2) for kc in range(16)]
POST_OFF = [0]
for _w in KCW:
    POST_OFF.append(POST_OFF[-1] + _w)
POST_W = POST_OFF[-1]
INV_SQRT_DH = 1.0 / float(np.sqrt(np.float64(DH)))
EPS = 1e-5
NEG_BIG = -1e30
FCN = 16
D3 = 3 * D


class Prog:

    ENGS = ("sync", "tensor", "vector", "scalar", "gpsimd")

    def __init__(self):
        self.ops = {e: [] for e in self.ENGS}
        self.count = {}
        self.lastw = {e: {} for e in self.ENGS}
        self.last_self = {}

    def op(self, eng, fn, sem=None, by=1):
        if eng in ("vector", "scalar"):
            self.wait(eng, self.last_self.get(eng))
        mk = None
        if sem is not None:
            self.count[sem] = self.count.get(sem, 0) + by
            mk = (sem, self.count[sem])
            if eng in ("vector", "scalar"):
                self.last_self[eng] = mk
        self.ops[eng].append(("op", fn, sem, by))
        return mk

    def wait(self, eng, mk):
        if mk is None:
            return
        sem, val = mk
        if self.lastw[eng].get(sem, 0) >= val:
            return
        self.lastw[eng][sem] = val
        self.ops[eng].append(("wait", sem, val))

    def waits(self, eng, mks):
        for mk in mks:
            self.wait(eng, mk)


class Buf:

    _n = 0

    def __init__(self, p, weng, newsem):
        self.p = p
        self.weng = weng
        self.ready = None
        self.uses = []
        self.newsem = newsem
        self.sem = None
        self.stsem = None

    def _mksem(self, tag):
        Buf._n += 1
        return self.newsem(f"b{Buf._n}{tag}")

    def begin_write(self):
        self.p.waits(self.weng, self.uses)
        self.uses = []

    def begin_write_on(self, eng):
        self.p.waits(eng, self.uses)
        self.uses = []

    def use(self, mk):
        if mk is not None:
            self.uses.append(mk)

    def load(self, dst, src, split=1):
        if self.sem is None:
            self.sem = self._mksem("l")
        self.begin_write()
        mk = None
        for i in range(split):
            mk = self.p.op("sync", (lambda d_, s_: (lambda e: e.dma_start(
                out=d_, in_=s_)))(dst[i] if split > 1 else dst,
                                  src[i] if split > 1 else src), self.sem, 16)
        self.ready = mk
        return mk

    def store(self, dst, src, deps=()):
        if self.stsem is None:
            self.stsem = self._mksem("s")
        self.p.waits("gpsimd", deps)
        mk = self.p.op("gpsimd", (lambda d_, s_: (lambda e: e.dma_start(
            out=d_, in_=s_)))(dst, src), self.stsem, 16)
        self.uses.append(mk)
        return mk

    def st_total(self):
        return (self.stsem, self.p.count.get(self.stsem, 0))


def build_program():
    nc = bass.Bass("TRN2", target_bir_lowering=False)

    xT = nc.dram_tensor("xT", [D, S], F32, kind="ExternalInput")
    xT_own = nc.dram_tensor("xT_own", [D, OWN], F32, kind="ExternalInput")
    x_own = nc.dram_tensor("x_own", [OWN, D], F32, kind="ExternalInput")
    wqkv = nc.dram_tensor("wqkv", [D, D3], F32, kind="ExternalInput")
    wo = nc.dram_tensor("wo", [D, D], F32, kind="ExternalInput")
    wT = nc.dram_tensor("wT", [D, D], F32, kind="ExternalInput")
    biascd = nc.dram_tensor("biascd", [128, 16], F32, kind="ExternalInput")
    rw1T = nc.dram_tensor("rw1T", [D, 64], F32, kind="ExternalInput")
    rb1d = nc.dram_tensor("rb1d", [64], F32, kind="ExternalInput")
    rw2T16 = nc.dram_tensor("rw2T16", [64, 16], F32, kind="ExternalInput")
    rb2d = nc.dram_tensor("rb2d", [16], F32, kind="ExternalInput")
    gatesd = nc.dram_tensor("gatesd", [16], F32, kind="ExternalInput")
    aT = nc.dram_tensor("aT", [D, 256], F32, kind="ExternalInput")
    bP = nc.dram_tensor("bP", [256, D], F32, kind="ExternalInput")
    alibid = nc.dram_tensor("alibid", [H * 128, POST_W], F32, kind="ExternalInput")
    identd = nc.dram_tensor("identd", [128, 128], F32, kind="ExternalInput")
    outd = nc.dram_tensor("out", [D, OWN], F32, kind="ExternalOutput")

    q_hbm = nc.dram_tensor("q_hbm", [D, OWN], F32)
    y_hbm = nc.dram_tensor("y_hbm", [D, OWN], F32)
    x1pre_hbm = nc.dram_tensor("x1pre_hbm", [OWN, D], F32)
    x1_hbm = nc.dram_tensor("x1_hbm", [OWN, D], F32)
    x2pre_hbm = nc.dram_tensor("x2pre_hbm", [OWN, D], F32)

    p = Prog()
    MM, DV, AC = "s_mm", "s_dv", "s_ac"

    def dap(t, offset, dims):
        return bass.AP(t, offset, dims)

    with ExitStack() as g:
        def sbuf(name, shape, dt=F32):
            return g.enter_context(nc.sbuf_tensor(name, shape, dt))

        sems = {}

        def newsem(name):
            sems[name] = g.enter_context(nc.semaphore(name))
            return name

        for s_ in (MM, DV, AC):
            newsem(s_)

        def B(weng):
            return Buf(p, weng, newsem)

        def phase_barrier(extra=()):
            mks = [(MM, p.count.get(MM, 0)), (DV, p.count.get(DV, 0)),
                   (AC, p.count.get(AC, 0))] + [m for m in extra if m[0]]
            for eng_ in Prog.ENGS:
                p.waits(eng_, mks)

        banks = [g.enter_context(nc.psum_tensor(f"bank{i}", [128, 512], F32))
                 for i in range(8)]
        bankbuf = [B("tensor") for _ in range(8)]

        ident_sb = sbuf("ident_sb", [128, 128])
        rb1_sb = sbuf("rb1_sb", [128, 1])
        rg_sb = sbuf("rg_sb", [128, 16])
        rb2_bc = sbuf("rb2_bc", [128, 16])
        gates_bc = sbuf("gates_bc", [128, 16])
        sums1 = sbuf("sums1", [128, 8])
        sq1 = sbuf("sq1", [128, 8])
        sums2 = sbuf("sums2", [128, 8])
        sq2 = sbuf("sq2", [128, 8])
        mean_sb = sbuf("mean_sb", [128, 8])
        rstd_sb = sbuf("rstd_sb", [128, 8])
        tmp1_sb = sbuf("tmp1_sb", [128, 8])
        tmp2_sb = sbuf("tmp2_sb", [128, 8])
        junk = sbuf("junk", [128, 512])
        acc1_sb = sbuf("acc1_sb", [128, 1])
        ones_sb = sbuf("ones_sb", [128, 128])

        identb, rb1b, rb2b, gatesb = B("sync"), B("sync"), B("sync"), B("sync")
        identmk = identb.load(ident_sb[:, :], identd.ap()[:, :])
        mk_rb1 = rb1b.load(rb1_sb[0:64, 0:1], dap(rb1d, 0, [[1, 64], [1, 1]]))
        mk_rb2 = rb2b.load(rb2_bc[:, :], dap(rb2d, 0, [[0, 128], [1, 16]]))
        mk_g = gatesb.load(gates_bc[:, :], dap(gatesd, 0, [[0, 128], [1, 16]]))
        p.waits("vector", [mk_rb2, mk_g])
        rgmk = p.op("vector", lambda e: e.tensor_add(rg_sb[:, :], rb2_bc[:, :],
                                                     gates_bc[:, :]), DV)
        for buf_ in (sums1, sq1, sums2, sq2):
            p.op("vector", (lambda b_: (lambda e: e.memset(b_[:, :], 0.0)))(buf_), DV)
        onesmk = p.op("vector", lambda e: e.memset(ones_sb[:, :], 1.0), DV)
        zeromk = (DV, p.count[DV])

        with ExitStack() as ph:
            def pb(name, shape, dt=F32):
                return ph.enter_context(nc.sbuf_tensor(name, shape, dt))

            xq = pb("xq", [128, FCN * 1024])
            xqb = B("sync")
            wqb = [pb(f"wqb{i}", [128, FCN * 128]) for i in range(2)]
            wqbuf = [B("sync"), B("sync")]
            qout = [pb(f"qout{i}", [128, 512]) for i in range(2)]
            qoutbuf = [B("vector"), B("vector")]

            xqmk = xqb.load(
                [dap(xq, fc0 * 1024, [[FCN * 1024, 128], [1024, 4], [1, 1024]])
                 for fc0 in range(0, FCN, 4)],
                [dap(xT_own, fc0 * 128 * OWN, [[OWN, 128], [128 * OWN, 4], [1, 1024]])
                 for fc0 in range(0, FCN, 4)], split=4)

            for oc in range(H):
                wb, wbuf = wqb[oc % 2], wqbuf[oc % 2]
                wmk = wbuf.load(
                    [dap(wb, fc0 * 128, [[FCN * 128, 128], [128, 4], [1, 128]])
                     for fc0 in range(0, FCN, 4)],
                    [dap(wqkv, fc0 * 128 * D3 + oc * 128,
                         [[D3, 128], [128 * D3, 4], [1, 128]])
                     for fc0 in range(0, FCN, 4)], split=4)
                for tb in range(2):
                    bi = (oc * 2 + tb) % 8
                    bb = bankbuf[bi]
                    bb.begin_write()
                    p.wait("tensor", wmk)
                    p.wait("tensor", xqmk)
                    for fc in range(FCN):
                        mmk = p.op("tensor", (lambda bi_, fc_, oc_, tb_: (lambda e: e.matmul(
                            banks[bi_][:, 0:512],
                            wqb[oc_ % 2][:, fc_ * 128:(fc_ + 1) * 128],
                            xq[:, fc_ * 1024 + tb_ * 512: fc_ * 1024 + tb_ * 512 + 512],
                            start=(fc_ == 0), stop=(fc_ == FCN - 1))))(bi, fc, oc, tb),
                            MM if fc == FCN - 1 else None)
                    wbuf.use(mmk)
                    bb.ready = mmk
                    qo, qob = qout[tb % 2], qoutbuf[tb % 2]
                    qob.begin_write()
                    p.wait("vector", mmk)
                    emk = p.op("vector", (lambda qo_, bi_: (lambda e: e.tensor_scalar(
                        qo_[:, :], banks[bi_][:, 0:512], INV_SQRT_DH, None,
                        ALU.mult)))(qo, bi), DV)
                    bb.use(emk)
                    qob.store(dap(q_hbm, oc * 128 * OWN + tb * 512,
                                  [[OWN, 128], [1, 512]]),
                              qo[:, :], deps=[emk])
            q_done = [qoutbuf[0].st_total(), qoutbuf[1].st_total()]
        phase_barrier(q_done)

        with ExitStack() as ph:
            def pb(name, shape, dt=F32):
                return ph.enter_context(nc.sbuf_tensor(name, shape, dt))

            xb = [pb(f"xb{i}", [128, FCN * 256]) for i in range(2)]
            xbuf = [B("sync"), B("sync")]
            wk_sb = [pb(f"wk{i}", [128, FCN * 256]) for i in range(2)]
            wv_sb = [pb(f"wv{i}", [128, FCN * 256]) for i in range(2)]
            wkbuf = [B("sync"), B("sync")]
            wvbuf = [B("sync"), B("sync")]
            K2 = [pb(f"K2_{i}", [128, 2 * 2048]) for i in range(2)]
            V2 = [pb(f"V2_{i}", [128, 16 * 256]) for i in range(2)]
            k2buf = [B("vector"), B("vector")]
            v2buf = [B("vector"), B("vector")]
            qh = [pb(f"qh{i}", [128, 1024]) for i in range(2)]
            qhbuf = [B("sync"), B("sync")]
            posb = [pb(f"posb{i}", [128, 1024]) for i in range(2)]
            posbuf = [B("sync"), B("sync")]
            st_sb = [pb(f"st_sb{i}", [128, 1024]) for i in range(2)]
            stbuf = [B("vector"), B("vector")]
            pt_sb = [pb(f"pt_sb{i}", [128, 1024]) for i in range(2)]
            ptbuf = [B("scalar"), B("scalar")]
            yfm_sg = pb("yfm_sg", [128, 1024])
            yfmsgbuf = B("vector")

            chunk_ctr = [0]
            kvg_ctr = [0]

            def kv_gen(g_):
                par = g_ % 2
                kbase = D + g_ * 256
                vbase = 2 * D + g_ * 256
                wkmk = wkbuf[par].load(
                    [dap(wk_sb[par], fc0 * 256, [[FCN * 256, 128], [256, 4], [1, 256]])
                     for fc0 in range(0, FCN, 4)],
                    [dap(wqkv, fc0 * 128 * D3 + kbase,
                         [[D3, 128], [128 * D3, 4], [1, 256]])
                     for fc0 in range(0, FCN, 4)], split=4)
                wvmk = wvbuf[par].load(
                    [dap(wv_sb[par], fc0 * 256, [[FCN * 256, 128], [256, 4], [1, 256]])
                     for fc0 in range(0, FCN, 4)],
                    [dap(wqkv, fc0 * 128 * D3 + vbase,
                         [[D3, 128], [128 * D3, 4], [1, 256]])
                     for fc0 in range(0, FCN, 4)], split=4)
                k2buf[par].begin_write()
                v2buf[par].begin_write()
                kv_last = None

                def xload(tb_):
                    return xbuf[tb_ % 2].load(
                        [dap(xb[tb_ % 2], fc0 * 256,
                             [[FCN * 256, 128], [256, 4], [1, 256]])
                         for fc0 in range(0, FCN, 4)],
                        [dap(xT, fc0 * 128 * S + tb_ * 256,
                             [[S, 128], [128 * S, 4], [1, 256]])
                         for fc0 in range(0, FCN, 4)], split=4)

                xmks = [xload(0), xload(1)] + [None] * 6
                yield
                for tb in range(8):
                    xbb = xbuf[tb % 2]
                    xmk = xmks[tb]
                    for oc in range(2):
                        bi = 6 + (kvg_ctr[0] % 2)
                        kvg_ctr[0] += 1
                        bb = bankbuf[bi]
                        bb.begin_write()
                        p.wait("tensor", wkmk)
                        p.wait("tensor", xmk)
                        for fc in range(FCN):
                            mmk = p.op("tensor", (lambda bi_, fc_, oc_, tb_, par_: (lambda e: e.matmul(
                                banks[bi_][:, 0:256],
                                wk_sb[par_][:, fc_ * 256 + oc_ * 128: fc_ * 256 + oc_ * 128 + 128],
                                xb[tb_ % 2][:, fc_ * 256:(fc_ + 1) * 256],
                                start=(fc_ == 0), stop=(fc_ == FCN - 1))))(bi, fc, oc, tb, par),
                                MM if fc == FCN - 1 else None)
                        xbb.use(mmk)
                        wkbuf[par].use(mmk)
                        bb.ready = mmk
                        p.wait("vector", mmk)
                        emk = p.op("vector", (lambda bi_, oc_, tb_, par_: (lambda e: e.tensor_copy(
                            K2[par_][:, oc_ * 2048 + tb_ * 256: oc_ * 2048 + tb_ * 256 + 256],
                            banks[bi_][:, 0:256])))(bi, oc, tb, par), DV)
                        bb.use(emk)
                        kv_last = emk
                        yield
                    for tt in range(2):
                        bi = 6 + (kvg_ctr[0] % 2)
                        kvg_ctr[0] += 1
                        bb = bankbuf[bi]
                        bb.begin_write()
                        p.wait("tensor", wvmk)
                        p.wait("tensor", xmk)
                        for fc in range(FCN):
                            mmk = p.op("tensor", (lambda bi_, fc_, tt_, tb_, par_: (lambda e: e.matmul(
                                banks[bi_][:, 0:256],
                                xb[tb_ % 2][:, fc_ * 256 + tt_ * 128: fc_ * 256 + tt_ * 128 + 128],
                                wv_sb[par_][:, fc_ * 256:(fc_ + 1) * 256],
                                start=(fc_ == 0), stop=(fc_ == FCN - 1))))(bi, fc, tt, tb, par),
                                MM if fc == FCN - 1 else None)
                        xbb.use(mmk)
                        wvbuf[par].use(mmk)
                        bb.ready = mmk
                        p.wait("vector", mmk)
                        emk = p.op("vector", (lambda bi_, tt_, tb_, par_: (lambda e: e.tensor_copy(
                            V2[par_][:, (2 * tb_ + tt_) * 256:(2 * tb_ + tt_) * 256 + 256],
                            banks[bi_][:, 0:256])))(bi, tt, tb, par), DV)
                        bb.use(emk)
                        kv_last = emk
                        if tt == 1 and tb + 2 < 8:
                            xmks[tb + 2] = xload(tb + 2)
                        yield
                k2buf[par].ready = kv_last
                v2buf[par].ready = kv_last

            def emit_pv(par, hd2, kc, exmk):
                s0 = kc // 2
                for half in range(2):
                    lo = max(half * 512, s0 * 128)
                    hi = (half + 1) * 512
                    if lo >= hi:
                        continue
                    wcols = hi - lo
                    ybi = 2 + half
                    dbi = 4 + half
                    last = (kc == 7) if half == 0 else (kc == 15)
                    if kc == 0:
                        bankbuf[ybi].begin_write()
                        bankbuf[dbi].begin_write()
                    p.wait("tensor", exmk)
                    p.wait("tensor", v2buf[par].ready)
                    p.wait("tensor", onesmk)
                    ro = lo - s0 * 128
                    co = lo - half * 512
                    pvmk = p.op("tensor", (lambda ybi_, kc_, hd2_, par_, co_, w_, ro_, st_, sp_: (
                        lambda e: e.matmul(
                            banks[ybi_][:, co_: co_ + w_],
                            V2[par_][:, kc_ * 256 + hd2_ * 128: kc_ * 256 + hd2_ * 128 + 128],
                            pt_sb[kc_ % 2][:, ro_: ro_ + w_],
                            start=st_, stop=sp_)))(ybi, kc, hd2, par, co, wcols,
                                                   ro, kc == 0, last), MM)
                    v2buf[par].use(pvmk)
                    ptbuf[kc % 2].use(pvmk)
                    dmk = p.op("tensor", (lambda dbi_, kc_, co_, w_, ro_, st_, sp_: (
                        lambda e: e.matmul(
                            banks[dbi_][:, co_: co_ + w_],
                            ones_sb[:, :],
                            pt_sb[kc_ % 2][:, ro_: ro_ + w_],
                            start=st_, stop=sp_)))(dbi, kc, co, wcols,
                                                   ro, kc == 0, last), MM)
                    ptbuf[kc % 2].use(dmk)
                    if last:
                        bankbuf[ybi].ready = pvmk
                        bankbuf[dbi].ready = dmk

            def attention(gh, gen):
                par = gh % 2
                yfmsgbuf.begin_write()
                for hd2 in range(2):
                    h = 2 * gh + hd2
                    qbuf = qhbuf[hd2 % 2]
                    p.waits("sync", q_done)
                    qmk = qbuf.load(qh[hd2 % 2][:, :],
                                    dap(q_hbm, h * 128 * OWN, [[OWN, 128], [1, 1024]]))
                    if gen is not None and hd2 == 0:
                        next(gen, None)
                    prev = None
                    for kc in range(16):
                        s0 = kc // 2
                        w = KCW[kc]
                        pbf = posbuf[kc % 2]
                        pmk = pbf.load(posb[kc % 2][:, 0:w],
                                       dap(alibid, h * 128 * POST_W + POST_OFF[kc],
                                           [[POST_W, 128], [1, w]]))
                        stb = stbuf[kc % 2]
                        stb.begin_write_on("scalar")
                        emks = []
                        nch = (w + 511) // 512
                        for ch in range(nch):
                            n = min(512, w - ch * 512)
                            bi = chunk_ctr[0] % 2
                            chunk_ctr[0] += 1
                            bb = bankbuf[bi]
                            bb.begin_write()
                            p.wait("tensor", k2buf[par].ready)
                            p.wait("tensor", qmk)
                            mmk = p.op("tensor", (lambda bi_, kc_, hd2_, par_, s0_, ch_, n_: (
                                lambda e: e.matmul(
                                    banks[bi_][:, 0:n_],
                                    K2[par_][:, hd2_ * 2048 + kc_ * 128: hd2_ * 2048 + kc_ * 128 + 128],
                                    qh[hd2_ % 2][:, s0_ * 128 + ch_ * 512: s0_ * 128 + ch_ * 512 + n_],
                                    start=True, stop=True)))(bi, kc, hd2, par, s0, ch, n),
                                MM)
                            k2buf[par].use(mmk)
                            qbuf.use(mmk)
                            bb.ready = mmk
                            p.wait("scalar", mmk)
                            emk = p.op("scalar", (lambda bi_, kc_, ch_, n_: (lambda e: e.copy(
                                st_sb[kc_ % 2][:, ch_ * 512: ch_ * 512 + n_],
                                banks[bi_][:, 0:n_])))(bi, kc, ch, n), AC)
                            bb.use(emk)
                            emks.append(emk)
                        if gen is not None:
                            next(gen, None)
                        if prev is not None:
                            emit_pv(par, hd2, *prev)
                        p.waits("vector", emks)
                        p.wait("vector", pmk)
                        stmk = p.op("vector", (lambda kc_, w_: (lambda e: e.tensor_add(
                            st_sb[kc_ % 2][:, 0:w_], st_sb[kc_ % 2][:, 0:w_],
                            posb[kc_ % 2][:, 0:w_])))(kc, w), DV)
                        pbf.use(stmk)
                        stb.use(stmk)
                        ptb = ptbuf[kc % 2]
                        ptb.begin_write()
                        p.wait("scalar", stmk)
                        exmk = p.op("scalar", (lambda kc_, w_: (lambda e: e.activation(
                            pt_sb[kc_ % 2][:, 0:w_], st_sb[kc_ % 2][:, 0:w_],
                            AF.Exp)))(kc, w), AC)
                        stb.use(exmk)
                        prev = (kc, exmk)
                    emit_pv(par, hd2, *prev)
                    yfmsgbuf.begin_write()
                    stbuf[0].begin_write_on("vector")
                    for half in range(2):
                        ybi = 2 + half
                        dbi = 4 + half
                        p.wait("vector", bankbuf[dbi].ready)
                        rmk = p.op("vector", (lambda dbi_, half_: (lambda e: e.reciprocal(
                            st_sb[0][:, half_ * 512:(half_ + 1) * 512],
                            banks[dbi_][:, 0:512])))(dbi, half), DV)
                        bankbuf[dbi].use(rmk)
                        stbuf[0].use(rmk)
                        p.wait("vector", bankbuf[ybi].ready)
                        ymk = p.op("vector", (lambda ybi_, half_: (lambda e: e.tensor_mul(
                            yfm_sg[:, half_ * 512: half_ * 512 + 512],
                            banks[ybi_][:, 0:512],
                            st_sb[0][:, half_ * 512:(half_ + 1) * 512])))(ybi, half), DV)
                        bankbuf[ybi].use(ymk)
                        stbuf[0].use(ymk)
                    last_y = (DV, p.count[DV])
                    yfmsgbuf.store(dap(y_hbm, (2 * gh + hd2) * 128 * OWN,
                                       [[OWN, 128], [1, 1024]]),
                                   dap(yfm_sg, 0, [[1024, 128], [1, 1024]]),
                                   deps=[last_y])

            gen0 = kv_gen(0)
            for _ in gen0:
                pass
            for gh in range(8):
                nxt = kv_gen(gh + 1) if gh < 7 else None
                attention(gh, nxt)
                if nxt is not None:
                    for _ in nxt:
                        pass
            y_done = [yfmsgbuf.st_total()]
        phase_barrier(y_done)
        with ExitStack() as cde:
            def cb(name, shape, dt=F32):
                return cde.enter_context(nc.sbuf_tensor(name, shape, dt))

            x1fm = cb("x1fm", [128, FCN * 1024])
            x1fmbuf = B("vector")
            wxaT = cb("wxaT", [128, 2 * 1024])
            wxaTbuf = B("vector")
            coeff = cb("coeff", [128, NSLOT * 16])
            mbc = cb("mbc", [128, 1024])
            rbc = cb("rbc", [128, 1024])
            rstd_tok = cb("rstd_tok", [128, 8])
            vt1 = cb("vt1", [128, 512])
            vt2 = cb("vt2", [128, 512])

            def emit_ln_fm(sumbanks, sqbanks):
                outmks = []
                for tb in range(2):
                    sb_, qb_ = sumbanks[tb], sqbanks[tb]
                    p.wait("vector", bankbuf[sb_].ready)
                    p.op("vector", (lambda sb2, tb_: (lambda e: e.tensor_scalar(
                        mbc[:, tb_ * 512:(tb_ + 1) * 512], banks[sb2][:, 0:512],
                        1.0 / D, None, ALU.mult)))(sb_, tb), DV)
                    mk0 = (DV, p.count[DV])
                    bankbuf[sb_].use(mk0)
                    p.wait("vector", bankbuf[qb_].ready)
                    p.op("vector", (lambda qb2, tb_: (lambda e: e.tensor_scalar(
                        vt1[:, :], banks[qb2][:, 0:512],
                        1.0 / D, None, ALU.mult)))(qb_, tb), DV)
                    mk1 = (DV, p.count[DV])
                    bankbuf[qb_].use(mk1)
                    p.op("vector", (lambda tb_: (lambda e: e.tensor_mul(
                        vt2[:, :], mbc[:, tb_ * 512:(tb_ + 1) * 512],
                        mbc[:, tb_ * 512:(tb_ + 1) * 512])))(tb), DV)
                    p.op("vector", lambda e: e.tensor_sub(vt1[:, :], vt1[:, :],
                                                          vt2[:, :]), DV)
                    vemk = p.op("vector", lambda e: e.tensor_scalar(
                        vt1[:, :], vt1[:, :], EPS, None, ALU.add), DV)
                    p.wait("scalar", vemk)
                    sqmk = p.op("scalar", lambda e: e.activation(
                        vt2[:, :], vt1[:, :], AF.Sqrt), AC)
                    p.wait("vector", sqmk)
                    p.op("vector", (lambda tb_: (lambda e: e.reciprocal(
                        rbc[:, tb_ * 512:(tb_ + 1) * 512], vt2[:, :])))(tb), DV)
                    rsl = (lambda tb_: rbc[:, tb_ * 512:(tb_ + 1) * 512])(tb)
                    mk_ = None
                    for _ in range(2):
                        p.op("vector", (lambda r_: (lambda e: e.tensor_mul(
                            vt2[:, :], r_, r_)))(rsl), DV)
                        p.op("vector", lambda e: e.tensor_mul(vt2[:, :], vt2[:, :],
                                                              vt1[:, :]), DV)
                        p.op("vector", lambda e: e.tensor_scalar(
                            vt2[:, :], vt2[:, :], -0.5, 1.5,
                            ALU.mult, ALU.add), DV)
                        mk_ = p.op("vector", (lambda r_: (lambda e: e.tensor_mul(
                            r_, r_, vt2[:, :])))(rsl), DV)
                    outmks.append(mk_)
                return outmks

            with ExitStack() as ph:
                def pb(name, shape, dt=F32):
                    return ph.enter_context(nc.sbuf_tensor(name, shape, dt))

                yfm_all = pb("yfm_all", [128, FCN * 1024])
                yball = [B("sync"), B("sync")]
                wocol = [pb(f"wocol{i}", [128, FCN * 128]) for i in range(2)]
                wocb = [B("sync"), B("sync")]
                xos = [pb(f"xos{i}", [128, 512]) for i in range(2)]
                xosb = [B("sync"), B("sync")]
                sqs = [pb(f"sqs{i}", [128, 512]) for i in range(2)]
                sqsb = [B("scalar"), B("scalar")]

                p.waits("sync", y_done)
                ymk_h = []
                for th in range(2):
                    mk = yball[th].load(
                        [dap(yfm_all, fc0 * 1024 + th * 512,
                             [[FCN * 1024, 128], [1024, 4], [1, 512]])
                         for fc0 in range(0, FCN, 4)],
                        [dap(y_hbm, fc0 * 128 * OWN + th * 512,
                             [[OWN, 128], [128 * OWN, 4], [1, 512]])
                         for fc0 in range(0, FCN, 4)], split=4)
                    ymk_h.append(mk)
                for b_ in (4, 5, 6, 7):
                    bankbuf[b_].begin_write()
                chmks = []
                for oc in range(FCN):
                    wbuf = wocb[oc % 2]
                    wmk = wbuf.load(
                        [dap(wocol[oc % 2], fc0 * 128,
                             [[FCN * 128, 128], [128, 4], [1, 128]])
                         for fc0 in range(0, FCN, 4)],
                        [dap(wo, fc0 * 128 * D + oc * 128,
                             [[D, 128], [128 * D, 4], [1, 128]])
                         for fc0 in range(0, FCN, 4)], split=4)
                    for tb in range(2):
                        bi = (oc * 2 + tb) % 2
                        bb = bankbuf[bi]
                        bb.begin_write()
                        p.wait("tensor", wmk)
                        p.wait("tensor", ymk_h[tb])
                        for fc in range(FCN):
                            mmk = p.op("tensor", (lambda bi_, fc_, oc_, tb_: (lambda e: e.matmul(
                                banks[bi_][:, 0:512],
                                wocol[oc_ % 2][:, fc_ * 128:(fc_ + 1) * 128],
                                yfm_all[:, fc_ * 1024 + tb_ * 512: fc_ * 1024 + tb_ * 512 + 512],
                                start=(fc_ == 0), stop=(fc_ == FCN - 1))))(bi, fc, oc, tb),
                                MM if fc == FCN - 1 else None)
                        wbuf.use(mmk)
                        yball[tb].use(mmk)
                        bb.ready = mmk
                        xo, xob = xos[tb], xosb[tb]
                        xmk = xob.load(xo[:, :],
                                       dap(xT_own, oc * 128 * OWN + tb * 512,
                                           [[OWN, 128], [1, 512]]))
                        p.wait("vector", mmk)
                        p.wait("vector", xmk)
                        cmk = p.op("vector", (lambda bi_, xo_, oc_, tb_: (lambda e: e.tensor_add(
                            x1fm[:, oc_ * 1024 + tb_ * 512: oc_ * 1024 + tb_ * 512 + 512],
                            banks[bi_][:, 0:512], xo_[:, :])))(bi, xo, oc, tb), DV)
                        bb.use(cmk)
                        xob.use(cmk)
                        chmks.append(cmk)
                        p.wait("tensor", cmk)
                        p.wait("tensor", onesmk)
                        smmk = p.op("tensor", (lambda oc_, tb_: (lambda e: e.matmul(
                            banks[4 + tb_][:, 0:512], ones_sb[:, :],
                            x1fm[:, oc_ * 1024 + tb_ * 512: oc_ * 1024 + tb_ * 512 + 512],
                            start=(oc_ == 0), stop=(oc_ == FCN - 1))))(oc, tb),
                            MM)
                        sq, sqb = sqs[tb], sqsb[tb]
                        sqb.begin_write()
                        p.wait("scalar", cmk)
                        sqmk = p.op("scalar", (lambda sq_, oc_, tb_: (lambda e: e.activation(
                            sq_[:, :],
                            x1fm[:, oc_ * 1024 + tb_ * 512: oc_ * 1024 + tb_ * 512 + 512],
                            AF.Square)))(sq, oc, tb), AC)
                        p.wait("tensor", sqmk)
                        qmmk = p.op("tensor", (lambda sq_, oc_, tb_: (lambda e: e.matmul(
                            banks[6 + tb_][:, 0:512], ones_sb[:, :], sq_[:, :],
                            start=(oc_ == 0), stop=(oc_ == FCN - 1))))(sq, oc, tb),
                            MM)
                        sqb.use(qmmk)
                        if oc == FCN - 1:
                            bankbuf[4 + tb].ready = smmk
                            bankbuf[6 + tb].ready = qmmk
                ln1mks = emit_ln_fm((4, 5), (6, 7))
                for tt in range(NSLOT):
                    bi = tt % 2
                    bb = bankbuf[bi]
                    bb.begin_write()
                    p.wait("tensor", ln1mks[1])
                    p.wait("tensor", identmk)
                    tmk = p.op("tensor", (lambda bi_, tt_: (lambda e: e.transpose(
                        banks[bi_][:, 0:128],
                        rbc[:, tt_ * 128:(tt_ + 1) * 128],
                        ident_sb[:, :])))(bi, tt), MM)
                    bb.ready = tmk
                    p.wait("vector", tmk)
                    rtk = p.op("vector", (lambda bi_, tt_: (lambda e: e.tensor_copy(
                        rstd_tok[:, tt_:tt_ + 1], banks[bi_][:, 0:1])))(bi, tt), DV)
                    bb.use(rtk)
                applymk = None
                for oc in range(FCN):
                    for tb in range(2):
                        p.wait("vector", ln1mks[tb])
                        applymk = p.op("vector", (lambda oc_, tb_: (lambda e: e.tensor_sub(
                            x1fm[:, oc_ * 1024 + tb_ * 512: oc_ * 1024 + tb_ * 512 + 512],
                            x1fm[:, oc_ * 1024 + tb_ * 512: oc_ * 1024 + tb_ * 512 + 512],
                            mbc[:, tb_ * 512:(tb_ + 1) * 512])))(oc, tb), DV)
                x1fmbuf.ready = applymk
            phase_barrier()

            with ExitStack() as ph:
                def pb(name, shape, dt=F32):
                    return ph.enter_context(nc.sbuf_tensor(name, shape, dt))

                rw1_sb = pb("rw1_sb", [128, FCN * 64])
                rw2_sb = pb("rw2_sb", [64, 16])
                h_fm = pb("h_fm", [64, 1024])
                logits = pb("logits", [128, 16])
                maskedL = pb("maskedL", [128, 16])
                eq1 = pb("eq1", [128, 16])
                eq2 = pb("eq2", [128, 16])
                m1s = pb("m1s", [128, 1])
                nm1 = pb("nm1", [128, 1])
                m2s = pb("m2s", [128, 1])
                e2s = pb("e2s", [128, 1])
                p1s = pb("p1s", [128, 1])
                p2s = pb("p2s", [128, 1])
                aT_sb = pb("aT_sb", [128, FCN * 256])
                xa_sb = pb("xa_sb", [128, 256])
                wxa_sb = pb("wxa_sb", [128, 256])
                hsc = pb("hsc", [128, 512])
                rw1b, rw2b, aTb = B("sync"), B("sync"), B("sync")

                r1mk = rw1b.load(
                    [dap(rw1_sb, fc0 * 64, [[FCN * 64, 128], [64, 4], [1, 64]])
                     for fc0 in range(0, FCN, 4)],
                    [dap(rw1T, fc0 * 128 * 64, [[64, 128], [128 * 64, 4], [1, 64]])
                     for fc0 in range(0, FCN, 4)], split=4)
                r2mk = rw2b.load(rw2_sb[:, :], dap(rw2T16, 0, [[16, 64], [1, 16]]))
                atmk = aTb.load(
                    [dap(aT_sb, fc0 * 256, [[FCN * 256, 128], [256, 4], [1, 256]])
                     for fc0 in range(0, FCN, 4)],
                    [dap(aT, fc0 * 128 * 256, [[256, 128], [128 * 256, 4], [1, 256]])
                     for fc0 in range(0, FCN, 4)], split=4)

                h_evacs = []
                for tb in range(2):
                    bi = 6 + tb % 2
                    bb = bankbuf[bi]
                    bb.begin_write()
                    p.wait("tensor", r1mk)
                    p.wait("tensor", x1fmbuf.ready)
                    for fc in range(FCN):
                        mmk = p.op("tensor", (lambda bi_, fc_, tb_: (lambda e: e.matmul(
                            banks[bi_][0:64, 0:512],
                            rw1_sb[:, fc_ * 64:(fc_ + 1) * 64],
                            x1fm[:, fc_ * 1024 + tb_ * 512: fc_ * 1024 + tb_ * 512 + 512],
                            start=(fc_ == 0), stop=(fc_ == FCN - 1))))(bi, fc, tb),
                            MM if fc == FCN - 1 else None)
                    x1fmbuf.use(mmk)
                    bb.ready = mmk
                    p.wait("vector", mmk)
                    if h_evacs:
                        p.wait("vector", h_evacs[-1])
                    hmmk = p.op("vector", (lambda bi_, tb_: (lambda e: e.tensor_mul(
                        hsc[0:64, :], banks[bi_][0:64, 0:512],
                        rbc[0:64, tb_ * 512:(tb_ + 1) * 512])))(bi, tb), DV)
                    bb.use(hmmk)
                    p.wait("scalar", hmmk)
                    p.wait("scalar", mk_rb1)
                    emk = p.op("scalar", (lambda tb_: (lambda e: e.activation(
                        h_fm[:, tb_ * 512:(tb_ + 1) * 512],
                        hsc[0:64, :], AF.Silu,
                        bias=rb1_sb[0:64, :], scale=1.0)))(tb), AC)
                    h_evacs.append(emk)

                cmk = None
                for tt in range(NSLOT):
                    bi = tt % 2
                    bb = bankbuf[bi]
                    bb.begin_write()
                    p.waits("tensor", h_evacs)
                    p.wait("tensor", r2mk)
                    lmk = p.op("tensor", (lambda bi_, tt_: (lambda e: e.matmul(
                        banks[bi_][:, 0:16],
                        h_fm[:, tt_ * 128:(tt_ + 1) * 128],
                        rw2_sb[:, :], start=True, stop=True)))(bi, tt), MM)
                    bb.ready = lmk
                    p.wait("vector", lmk)
                    p.wait("vector", rgmk)
                    lgmk = p.op("vector", (lambda bi_: (lambda e: e.tensor_add(
                        logits[:, :], banks[bi_][:, 0:16], rg_sb[:, :])))(bi), DV)
                    bb.use(lgmk)
                    p.op("vector", lambda e: e.reduce_max(nm1[:, :], logits[:, :],
                                                          axis=AX.X, negate=True), DV)
                    p.op("vector", lambda e: e.tensor_scalar(
                        m1s[:, :], nm1[:, :], -1.0, None, ALU.mult), DV)
                    p.op("vector", lambda e: e.tensor_scalar(
                        eq1[:, :], logits[:, :], m1s[:, :], None, ALU.is_equal), DV)
                    p.op("vector", lambda e: e.scalar_tensor_tensor(
                        maskedL[:, :], eq1[:, :], NEG_BIG, logits[:, :],
                        ALU.mult, ALU.add), DV)
                    m2mk = p.op("vector", lambda e: e.reduce_max(
                        m2s[:, :], maskedL[:, :], axis=AX.X), DV)
                    p.wait("scalar", m2mk)
                    e2mk = p.op("scalar", lambda e: e.activation(
                        e2s[:, :], m2s[:, :], AF.Exp, bias=nm1[:, :], scale=1.0), AC)
                    p.wait("vector", e2mk)
                    p.op("vector", lambda e: e.tensor_scalar(
                        p1s[:, :], e2s[:, :], 1.0, None, ALU.add), DV)
                    p.op("vector", lambda e: e.reciprocal(p1s[:, :], p1s[:, :]), DV)
                    p.op("vector", lambda e: e.tensor_mul(p2s[:, :], e2s[:, :],
                                                          p1s[:, :]), DV)
                    p.op("vector", lambda e: e.tensor_scalar(
                        eq2[:, :], maskedL[:, :], m2s[:, :], None, ALU.is_equal), DV)
                    p.op("vector", lambda e: e.tensor_scalar(
                        eq1[:, :], eq1[:, :], p1s[:, :], None, ALU.mult), DV)
                    cmk = p.op("vector", (lambda tt_: (lambda e: e.scalar_tensor_tensor(
                        coeff[:, tt_ * 16:(tt_ + 1) * 16], eq2[:, :], p2s[:, :],
                        eq1[:, :], ALU.mult, ALU.add)))(tt), DV)

                wxa_last = None
                for tt in range(NSLOT):
                    bi = 2 + tt % 2
                    bb = bankbuf[bi]
                    bb.begin_write()
                    p.wait("tensor", atmk)
                    p.wait("tensor", x1fmbuf.ready)
                    for fc in range(FCN):
                        mmk = p.op("tensor", (lambda bi_, fc_, tt_: (lambda e: e.matmul(
                            banks[bi_][:, 0:256],
                            x1fm[:, fc_ * 1024 + tt_ * 128: fc_ * 1024 + tt_ * 128 + 128],
                            aT_sb[:, fc_ * 256:(fc_ + 1) * 256],
                            start=(fc_ == 0), stop=(fc_ == FCN - 1))))(bi, fc, tt),
                            MM if fc == FCN - 1 else None)
                    x1fmbuf.use(mmk)
                    bb.ready = mmk
                    p.wait("vector", mmk)
                    xamk = p.op("vector", (lambda bi_, tt_: (lambda e: e.tensor_scalar(
                        xa_sb[:, :], banks[bi_][:, 0:256],
                        rstd_tok[:, tt_:tt_ + 1], None, ALU.mult)))(bi, tt), DV)
                    bb.use(xamk)
                    p.wait("vector", cmk)
                    wmk = None
                    for ex in range(16):
                        wmk = p.op("vector", (lambda ex_, tt_: (lambda e: e.tensor_scalar(
                            wxa_sb[:, ex_ * 16:(ex_ + 1) * 16],
                            xa_sb[:, ex_ * 16:(ex_ + 1) * 16],
                            coeff[:, tt_ * 16 + ex_: tt_ * 16 + ex_ + 1],
                            None, ALU.mult)))(ex, tt), DV)
                    for c in range(2):
                        bi2 = 4 + c
                        bb2 = bankbuf[bi2]
                        bb2.begin_write()
                        p.wait("tensor", wmk)
                        p.wait("tensor", identmk)
                        tmk = p.op("tensor", (lambda bi2_, c_: (lambda e: e.transpose(
                            banks[bi2_][:, 0:128],
                            wxa_sb[:, c_ * 128:(c_ + 1) * 128],
                            ident_sb[:, :])))(bi2, c), MM)
                        bb2.ready = tmk
                        p.wait("vector", tmk)
                        emk = p.op("vector", (lambda bi2_, c_, tt_: (lambda e: e.tensor_copy(
                            wxaT[:, c_ * 1024 + tt_ * 128: c_ * 1024 + tt_ * 128 + 128],
                            banks[bi2_][:, 0:128])))(bi2, c, tt), DV)
                        bb2.use(emk)
                        wxa_last = emk
                wxaTbuf.ready = wxa_last
            phase_barrier()

            with ExitStack() as ph:
                def pb(name, shape, dt=F32):
                    return ph.enter_context(nc.sbuf_tensor(name, shape, dt))

                x2fm = pb("x2fm", [128, FCN * 1024])
                wtcol = [pb(f"wtcol{i}", [128, FCN * 128]) for i in range(2)]
                wtcb = [B("sync"), B("sync")]
                bP_sb = pb("bP_sb", [128, 2 * 2048])
                bias_v = pb("bias_v", [128, 16])
                sqs = [pb(f"sqsE{i}", [128, 512]) for i in range(2)]
                sqsb = [B("scalar"), B("scalar")]
                x2buf = B("vector")
                bPb, biasb = B("sync"), B("sync")

                bpmk = bPb.load([dap(bP_sb, c * 2048, [[2 * 2048, 128], [1, 2048]])
                                 for c in range(2)],
                                [dap(bP, c * 128 * D, [[D, 128], [1, 2048]])
                                 for c in range(2)], split=2)
                bvmk = biasb.load(bias_v[:, :], dap(biascd, 0, [[16, 128], [1, 16]]))

                for b_ in (4, 5, 6, 7):
                    bankbuf[b_].begin_write()
                for oc in range(FCN):
                    wbuf = wtcb[oc % 2]
                    wmk = wbuf.load(
                        [dap(wtcol[oc % 2], fc0 * 128,
                             [[FCN * 128, 128], [128, 4], [1, 128]])
                         for fc0 in range(0, FCN, 4)],
                        [dap(wT, fc0 * 128 * D + oc * 128,
                             [[D, 128], [128 * D, 4], [1, 128]])
                         for fc0 in range(0, FCN, 4)], split=4)
                    for tb in range(2):
                        bi = (oc * 2 + tb) % 2
                        bb = bankbuf[bi]
                        bb.begin_write()
                        p.wait("tensor", wmk)
                        p.wait("tensor", x1fmbuf.ready)
                        p.wait("tensor", wxaTbuf.ready)
                        p.wait("tensor", bpmk)
                        for fc in range(FCN):
                            mmk = p.op("tensor", (lambda bi_, fc_, oc_, tb_: (lambda e: e.matmul(
                                banks[bi_][:, 0:512],
                                wtcol[oc_ % 2][:, fc_ * 128:(fc_ + 1) * 128],
                                x1fm[:, fc_ * 1024 + tb_ * 512: fc_ * 1024 + tb_ * 512 + 512],
                                start=(fc_ == 0), stop=(fc_ == FCN - 1))))(bi, fc, oc, tb),
                                MM if fc == FCN - 1 else None)
                        bb.ready = mmk
                        wbuf.use(mmk)
                        x1fmbuf.use(mmk)
                        di = 2 + bi
                        db = bankbuf[di]
                        db.begin_write()
                        p.wait("tensor", wxaTbuf.ready)
                        p.wait("tensor", bpmk)
                        for c in range(2):
                            dmk = p.op("tensor", (lambda di_, c_, oc_, tb_: (lambda e: e.matmul(
                                banks[di_][:, 0:512],
                                bP_sb[:, c_ * 2048 + oc_ * 128: c_ * 2048 + oc_ * 128 + 128],
                                wxaT[:, c_ * 1024 + tb_ * 512: c_ * 1024 + tb_ * 512 + 512],
                                start=(c_ == 0), stop=(c_ == 1))))(di, c, oc, tb),
                                MM if c == 1 else None)
                        db.ready = dmk
                        wxaTbuf.use(dmk)
                        p.wait("vector", mmk)
                        p.wait("vector", dmk)
                        p.wait("vector", bvmk)
                        p.op("vector", (lambda bi_, oc_, tb_: (lambda e: e.tensor_add(
                            x2fm[:, oc_ * 1024 + tb_ * 512: oc_ * 1024 + tb_ * 512 + 512],
                            banks[bi_][:, 0:512],
                            x1fm[:, oc_ * 1024 + tb_ * 512: oc_ * 1024 + tb_ * 512 + 512])))(bi, oc, tb), DV)
                        bb.use((DV, p.count[DV]))
                        p.op("vector", (lambda oc_, tb_: (lambda e: e.tensor_mul(
                            x2fm[:, oc_ * 1024 + tb_ * 512: oc_ * 1024 + tb_ * 512 + 512],
                            x2fm[:, oc_ * 1024 + tb_ * 512: oc_ * 1024 + tb_ * 512 + 512],
                            rbc[:, tb_ * 512:(tb_ + 1) * 512])))(oc, tb), DV)
                        cmk = p.op("vector", (lambda di_, oc_, tb_: (lambda e: e.scalar_tensor_tensor(
                            x2fm[:, oc_ * 1024 + tb_ * 512: oc_ * 1024 + tb_ * 512 + 512],
                            banks[di_][:, 0:512], bias_v[:, oc_:oc_ + 1],
                            x2fm[:, oc_ * 1024 + tb_ * 512: oc_ * 1024 + tb_ * 512 + 512],
                            ALU.add, ALU.add)))(di, oc, tb), DV)
                        db.use(cmk)
                        x2buf.use(cmk)
                        p.wait("tensor", cmk)
                        p.wait("tensor", onesmk)
                        smmk = p.op("tensor", (lambda oc_, tb_: (lambda e: e.matmul(
                            banks[4 + tb_][:, 0:512], ones_sb[:, :],
                            x2fm[:, oc_ * 1024 + tb_ * 512: oc_ * 1024 + tb_ * 512 + 512],
                            start=(oc_ == 0), stop=(oc_ == FCN - 1))))(oc, tb),
                            MM)
                        sq, sqb = sqs[tb], sqsb[tb]
                        sqb.begin_write()
                        p.wait("scalar", cmk)
                        sqmk = p.op("scalar", (lambda sq_, oc_, tb_: (lambda e: e.activation(
                            sq_[:, :],
                            x2fm[:, oc_ * 1024 + tb_ * 512: oc_ * 1024 + tb_ * 512 + 512],
                            AF.Square)))(sq, oc, tb), AC)
                        p.wait("tensor", sqmk)
                        qmmk = p.op("tensor", (lambda sq_, oc_, tb_: (lambda e: e.matmul(
                            banks[6 + tb_][:, 0:512], ones_sb[:, :], sq_[:, :],
                            start=(oc_ == 0), stop=(oc_ == FCN - 1))))(sq, oc, tb),
                            MM)
                        sqb.use(qmmk)
                        if oc == FCN - 1:
                            bankbuf[4 + tb].ready = smmk
                            bankbuf[6 + tb].ready = qmmk
                ln2mks = emit_ln_fm((4, 5), (6, 7))
                for oc in range(FCN):
                    for tb in range(2):
                        p.wait("vector", ln2mks[tb])
                        p.op("vector", (lambda oc_, tb_: (lambda e: e.tensor_sub(
                            x2fm[:, oc_ * 1024 + tb_ * 512: oc_ * 1024 + tb_ * 512 + 512],
                            x2fm[:, oc_ * 1024 + tb_ * 512: oc_ * 1024 + tb_ * 512 + 512],
                            mbc[:, tb_ * 512:(tb_ + 1) * 512])))(oc, tb), DV)
                        applymk = p.op("vector", (lambda oc_, tb_: (lambda e: e.tensor_mul(
                            x2fm[:, oc_ * 1024 + tb_ * 512: oc_ * 1024 + tb_ * 512 + 512],
                            x2fm[:, oc_ * 1024 + tb_ * 512: oc_ * 1024 + tb_ * 512 + 512],
                            rbc[:, tb_ * 512:(tb_ + 1) * 512])))(oc, tb), DV)
                out_mks = []
                for q4 in range(4):
                    mk = x2buf.store(
                        dap(outd, q4 * 4 * 128 * OWN, [[OWN, 128], [128 * OWN, 4], [1, 1024]]),
                        dap(x2fm, q4 * 4 * 1024, [[FCN * 1024, 128], [1024, 4], [1, 1024]]),
                        deps=[applymk])
                out_done = [x2buf.st_total()]

        for sem_, val_ in out_done:
            p.ops["gpsimd"].append(("wait", sem_, val_))

        block = g.enter_context(nc.Block())

        def make_runner(eng_name):
            lst = p.ops[eng_name]

            def run(e):
                for item in lst:
                    if item[0] == "wait":
                        e.wait_ge(sems[item[1]], item[2])
                    else:
                        _, fn, sem, by = item
                        ins = fn(e)
                        if sem is not None:
                            ins.then_inc(sems[sem], by)
            return run

        block.sync(make_runner("sync"))
        block.tensor(make_runner("tensor"))
        block.vector(make_runner("vector"))
        block.scalar(make_runner("scalar"))
        block.gpsimd(make_runner("gpsimd"))

    return nc


_NC_CACHE = []


def _build_alibi(r):
    slopes = (2.0 ** (-8.0 / np.arange(1, H + 1, dtype=np.float32))).astype(np.float32)
    tab = np.full((H * 128, POST_W), -1e30, dtype=np.float32)
    tiles = np.asarray(TILES[r])
    for kc in range(16):
        s0 = kc // 2
        w = KCW[kc]
        k = (kc * 128 + np.arange(128))[:, None]
        j = np.arange(w)[None, :]
        scol = s0 + j // 128
        q = tiles[scol] * 128 + (j % 128)
        pos = (k - q).astype(np.float32)
        valid = k <= q
        for h in range(H):
            blockv = np.where(valid, slopes[h] * pos, np.float32(-1e30))
            tab[h * 128:(h + 1) * 128, POST_OFF[kc]:POST_OFF[kc] + w] = blockv
    return tab


def kernel(L0_attn_x, L0_attn_W_qkv, L0_attn_W_o, L0_ffn_W, L0_ffn_bias,
           L0_ffn_rw1, L0_ffn_rb1, L0_ffn_rw2, L0_ffn_rb2, L0_ffn_A,
           L0_ffn_B, L0_ffn_gates):
    x = np.asarray(L0_attn_x, dtype=np.float32)
    wqkv = np.ascontiguousarray(np.asarray(L0_attn_W_qkv, dtype=np.float32))
    wo = np.ascontiguousarray(np.asarray(L0_attn_W_o, dtype=np.float32))
    wT = np.ascontiguousarray(np.asarray(L0_ffn_W, dtype=np.float32).T)
    bias_cols = np.ascontiguousarray(
        np.asarray(L0_ffn_bias, dtype=np.float32).reshape(16, 128).T)
    rw1T = np.ascontiguousarray(np.asarray(L0_ffn_rw1, dtype=np.float32).T)
    rb1 = np.ascontiguousarray(np.asarray(L0_ffn_rb1, dtype=np.float32))
    rw2T16 = np.ascontiguousarray(np.asarray(L0_ffn_rw2, dtype=np.float32).T[:, :16])
    rb2_16 = np.ascontiguousarray(np.asarray(L0_ffn_rb2, dtype=np.float32)[:16])
    aT = np.ascontiguousarray(np.asarray(L0_ffn_A, dtype=np.float32).reshape(256, D).T)
    bP = np.ascontiguousarray(
        np.asarray(L0_ffn_B, dtype=np.float32).transpose(0, 2, 1).reshape(256, D))
    gates = np.ascontiguousarray(np.asarray(L0_ffn_gates, dtype=np.float32))
    ident = np.eye(128, dtype=np.float32)

    alibi_r = [_build_alibi(0), _build_alibi(1)]
    own_rows_r = []
    for r in range(2):
        rows = np.concatenate([np.arange(128 * t, 128 * t + 128) for t in TILES[r]])
        own_rows_r.append(rows)

    if not _NC_CACHE:
        _NC_CACHE.append(build_program())
    nc = _NC_CACHE[0]

    in_maps = []
    for c in range(8):
        b, r = c // 2, c % 2
        xb = np.ascontiguousarray(x[b])
        xTb = np.ascontiguousarray(xb.T)
        rows = own_rows_r[r]
        in_maps.append({
            "xT": xTb,
            "xT_own": np.ascontiguousarray(xTb[:, rows]),
            "x_own": np.ascontiguousarray(xb[rows]),
            "wqkv": wqkv, "wo": wo, "wT": wT, "biascd": bias_cols,
            "rw1T": rw1T, "rb1d": rb1, "rw2T16": rw2T16, "rb2d": rb2_16,
            "gatesd": gates, "aT": aT, "bP": bP,
            "alibid": alibi_r[r], "identd": ident,
        })

    res = run_bass_kernel_spmd(nc, in_maps, core_ids=list(range(8)))

    out = np.empty((NB, S, D), dtype=np.float32)
    for c in range(8):
        b, r = c // 2, c % 2
        co = np.asarray(res.results[c]["out"]).T
        for s_, tile in enumerate(TILES[r]):
            out[b, 128 * tile:128 * tile + 128, :] = co[128 * s_:128 * (s_ + 1), :]
    return out
